# version 50
# baseline (speedup 1.0000x reference)
import numpy as np
import ml_dtypes
import concourse.bass as bass
import concourse.mybir as mybir
from concourse.bass_utils import run_bass_kernel_spmd

F32 = mybir.dt.float32
BF16 = mybir.dt.bfloat16
AF = mybir.ActivationFunctionType
ALU = mybir.AluOpType
AX = mybir.AxisListType
BF = ml_dtypes.bfloat16

T = 2048
D = 1024
NH = 16
DH = 64
E = 8
HF = 4096
NTT = 4
NDC = 8
NTC = 16
NJ = 32

PHASES = ("ln", "gate", "qkv", "attn", "ffn", "final")
LAST_PHASE = "final"
DEBUG_OUT = None


ENGINES = ("sync", "pe", "act", "dve", "pool")


SAFE_DIST = 2


class _Op:
    __slots__ = ("eng", "fn", "deps", "marks", "idx", "inc_val", "tag", "drain",
                 "sem")

    def __init__(self, eng, fn, tag):
        self.eng = eng
        self.fn = fn
        self.deps = set()
        self.marks = []
        self.idx = -1
        self.inc_val = 0
        self.tag = tag
        self.drain = False
        self.sem = None


class Sched:
    def __init__(self, nc):
        self.nc = nc
        self.sem = {e: nc.alloc_semaphore("sched_" + e) for e in ENGINES}
        self.count = {e: 0 for e in ENGINES}
        self.dma_sems = {}
        self.ops = []
        self.writer = {}
        self.readers = {}
        self.nblk = 0

    def _dep(self, op, item):
        if isinstance(item, _Op):
            if item.eng == op.eng:
                op.deps.add(item)
            else:
                op.deps.add(item)
        else:
            op.marks.append(item)

    def add(self, eng, fn, reads=(), writes=(), tag=""):
        op = _Op(eng, fn, tag)
        for t in reads:
            w = self.writer.get(t)
            if w is not None:
                self._dep(op, w)
            self.readers.setdefault(t, []).append(op)
        for t in writes:
            w = self.writer.get(t)
            if w is not None:
                self._dep(op, w)
            for r in self.readers.get(t, ()):
                if r is not op:
                    self._dep(op, r)
            self.writer[t] = op
            self.readers[t] = []
        self.ops.append(op)
        return op

    def dma(self, fn, reads=(), writes=(), tag="", eng="sync"):
        op = self.add(eng, fn, reads, writes, tag)
        key = writes[0] if writes else tag
        if key not in self.dma_sems:
            self.dma_sems[key] = [self.nc.alloc_semaphore("dma_" + key), 0]
        op.sem = self.dma_sems[key]
        return op

    def flush(self, name):
        ops = self.ops
        per = {e: [] for e in ENGINES}
        for op in ops:
            per[op.eng].append(op)
            op.idx = len(per[op.eng]) - 1
        need = set()
        for op in ops:
            for d in op.deps:
                if isinstance(d, _Op) and d.eng != op.eng:
                    need.add(d)
        for e in ENGINES:
            for op in per[e]:
                if op.sem is not None:
                    op.sem[1] += 16
                    op.inc_val = op.sem[1]
                elif op in need:
                    self.count[e] += 1
                    op.inc_val = self.count[e]
        for e in ("act", "dve", "pool"):
            drained = -1
            for op in per[e]:
                for d in op.deps:
                    if d.eng == e and d.idx > drained and op.idx - d.idx < SAFE_DIST:
                        op.drain = True
                        drained = op.idx - 1
                        break
        blk = self.nc.Block("blk%d_%s" % (self.nblk, name))
        self.nblk += 1
        sched = self

        def emit(eng_name):
            def body(eng):
                seen = {}
                for op in per[eng_name]:
                    waits = {}
                    for d in op.deps:
                        if d.eng != eng_name and d.inc_val:
                            s = d.sem[0] if d.sem is not None else sched.sem[d.eng]
                            waits[s] = max(waits.get(s, 0), d.inc_val)
                    for (s, v) in op.marks:
                        waits[s] = max(waits.get(s, 0), v)
                    for s, v in waits.items():
                        if seen.get(s, -1) >= v:
                            continue
                        seen[s] = v
                        eng.wait_ge(s, v)
                    if op.drain:
                        eng.drain()
                    instr = op.fn(eng)
                    if op.inc_val:
                        if op.sem is not None:
                            instr.then_inc(op.sem[0], 16)
                        else:
                            instr.then_inc(sched.sem[eng_name], 1)
            return body

        with blk:
            if per["sync"]:
                blk.sync(emit("sync"))
            if per["pe"]:
                blk.tensor(emit("pe"))
            if per["act"]:
                blk.scalar(emit("act"))
            if per["dve"]:
                blk.vector(emit("dve"))
            if per["pool"]:
                blk.gpsimd(emit("pool"))
        neww, newr = {}, {}
        for t, w in self.writer.items():
            if isinstance(w, _Op):
                if w.eng == "sync" and w.inc_val:
                    neww[t] = (w.sem[0], w.inc_val)
            else:
                neww[t] = w
        self.writer = neww
        self.readers = newr
        self.ops = []



def build_program():
    nc = bass.Bass("TRN2", target_bir_lowering=False)
    s = Sched(nc)

    d_xT = nc.dram_tensor("xT", [D, T], F32, kind="ExternalInput").ap()
    d_qk = nc.dram_tensor("wqkS", [16, 128, 1024], F32, kind="ExternalInput").ap()
    d_wv = nc.dram_tensor("wvS", [8, 128, 1024], F32, kind="ExternalInput").ap()
    d_wo = nc.dram_tensor("woutS", [8, 128, 1024], F32, kind="ExternalInput").ap()
    d_wg = nc.dram_tensor("wgS", [128, 64], F32, kind="ExternalInput").ap()
    d_w1 = nc.dram_tensor("w1S", [E, NJ, 128, 1024], F32, kind="ExternalInput").ap()
    d_wf = nc.dram_tensor("wfS", [16, 128, 1024], F32, kind="ExternalInput").ap()
    d_b1 = nc.dram_tensor("b1S", [128, 256], F32, kind="ExternalInput").ap()
    d_bo = nc.dram_tensor("boS", [8, 1024], F32, kind="ExternalInput").ap()
    d_cn = nc.dram_tensor("constS", [128, 1410], BF16, kind="ExternalInput").ap()
    d_tab = nc.dram_tensor("tabS", [4, 128, T], BF16, kind="ExternalInput").ap()
    d_cbs = nc.dram_tensor("cbsScratch", [E, 128, T], BF16, kind="Internal").ap()
    d_out = nc.dram_tensor("outT", [D, T], F32, kind="ExternalOutput").ap()
    dbg_out = None
    if DEBUG_OUT:
        dbg_out = nc.dram_tensor("dbgT", [4096, 2560], F32, kind="ExternalOutput").ap()

    C_I, C_R, C_ONE, C_HS, C_SEL = 0, 128, 256, 384, 386

    xn = [nc.alloc_sbuf_tensor("xn%d" % c, [128, T], BF16) for c in range(NDC)]
    combT = nc.alloc_sbuf_tensor("combT", [8, T], BF16)
    b1_all = nc.alloc_sbuf_tensor("b1all", [128, 256], F32)
    boRep = nc.alloc_sbuf_tensor("boRep", [8, 1024], BF16)
    consts = nc.alloc_sbuf_tensor("consts", [128, 1410], BF16)
    acc = [nc.alloc_sbuf_tensor("acc%d" % c, [128, T], BF16) for c in range(NDC)]
    wgb = nc.alloc_sbuf_tensor("wgb", [128, 64], BF16)
    eps_t = nc.alloc_sbuf_tensor("epsT", [128, 1], F32)
    bias16 = nc.alloc_sbuf_tensor("bias16", [128, 16], F32)
    rs_f = nc.alloc_sbuf_tensor("rsf", [1, 512], F32)
    rs_b = nc.alloc_sbuf_tensor("rsb", [1, 512], BF16)
    bc_sb = nc.alloc_sbuf_tensor("bcsb", [64, 512], F32)

    ident = consts[:, C_I:C_I + 128]
    rotT = consts[:, C_R:C_R + 128]
    ones128 = consts[:, C_ONE:C_ONE + 128]
    onescol64 = consts[0:64, C_ONE:C_ONE + 1]
    onesrow64 = consts[0:1, C_ONE:C_ONE + 64]
    sel8 = consts[0:8, C_SEL:C_SEL + 1024]

    phase_on = {p: True for p in PHASES}
    seen_last = False
    for p in PHASES:
        phase_on[p] = not seen_last
        if p == LAST_PHASE:
            seen_last = True

    s.dma(lambda e: e.dma_start(consts[:], d_cn[:]), writes=["consts"])
    s.dma(lambda e: e.dma_start(b1_all[:], d_b1[:]), writes=["b1all"])

    g_xf = [nc.sbuf_tensor("xf%d" % c, [128, T], F32) for c in range(NDC)]
    xf = [g.__enter__() for g in g_xf]
    g_more = [
        nc.sbuf_tensor("xb0", [128, T], BF16), nc.sbuf_tensor("xb1", [128, T], BF16),
        nc.sbuf_tensor("xq0", [128, T], BF16), nc.sbuf_tensor("xq1", [128, T], BF16),
        nc.sbuf_tensor("meanS", [128, T], F32), nc.sbuf_tensor("rstdS", [128, T], F32),
        nc.sbuf_tensor("vscr", [128, 512], F32), nc.sbuf_tensor("vscr2", [128, 512], F32),
        nc.sbuf_tensor("wgf", [128, 64], F32), nc.sbuf_tensor("bof", [8, 1024], F32),
    ]
    (xb0, xb1, xq0, xq1, meanS, rstdS, vscr, vscr2, wgf, bof) = [g.__enter__() for g in g_more]
    xb = [xb0, xb1]
    xq = [xq0, xq1]

    s.dma(lambda e: e.dma_start(wgf[:], d_wg[:]), writes=["wgf"])
    s.add("pool", lambda e: e.tensor_copy(wgb[:], wgf[:]), reads=["wgf"], writes=["wgb"])
    s.dma(lambda e: e.dma_start(bof[:], d_bo[:]), writes=["bof"])
    s.add("pool", lambda e: e.tensor_copy(boRep[:], bof[:]), reads=["bof"], writes=["boRep"])
    s.add("pool", lambda e: e.memset(eps_t[:], 1e-5), writes=["epsT"])

    g_lnps = [nc.psum_tensor("lnps%d" % i, [128, 512], F32) for i in range(8)]
    lnps = [g.__enter__() for g in g_lnps]

    for c in range(NDC):
        s.dma(lambda e, c=c: e.dma_start(xf[c][:], d_xT[c * 128:(c + 1) * 128, :]),
              writes=["xf%d" % c])
        s.add("dve", lambda e, c=c: e.tensor_copy(xb[c % 2][:], xf[c][:]),
              reads=["xf%d" % c], writes=["xb%d" % (c % 2)])
        s.add("act", lambda e, c=c: e.activation(xq[c % 2][:], xf[c][:], AF.Square),
              reads=["xf%d" % c], writes=["xq%d" % (c % 2)])
        for tt in range(NTT):
            st, sp = (c == 0), (c == NDC - 1)
            s.add("pe", lambda e, c=c, tt=tt, st=st, sp=sp: e.matmul(
                lnps[tt * 2][:], ones128, xb[c % 2][:, tt * 512:(tt + 1) * 512],
                start=st, stop=sp),
                reads=["xb%d" % (c % 2), "consts"], writes=["ps_ln%d" % (tt * 2)])
            s.add("pe", lambda e, c=c, tt=tt, st=st, sp=sp: e.matmul(
                lnps[tt * 2 + 1][:], ones128, xq[c % 2][:, tt * 512:(tt + 1) * 512],
                start=st, stop=sp),
                reads=["xq%d" % (c % 2), "consts"], writes=["ps_ln%d" % (tt * 2 + 1)])
    for tt in range(NTT):
        sl = slice(tt * 512, (tt + 1) * 512)
        s.add("dve", lambda e, tt=tt, sl=sl: e.tensor_scalar_mul(
            meanS[:, sl], lnps[tt * 2][:], 1.0 / D),
            reads=["ps_ln%d" % (tt * 2)], writes=["mean"])
        s.add("dve", lambda e, tt=tt, sl=sl: e.tensor_mul(
            vscr[:], meanS[:, sl], meanS[:, sl]), reads=["mean"], writes=["vscr"])
        s.add("dve", lambda e, tt=tt: e.scalar_tensor_tensor(
            vscr2[:], lnps[tt * 2 + 1][:], 1.0 / D, vscr[:], ALU.mult, ALU.subtract),
            reads=["ps_ln%d" % (tt * 2 + 1), "vscr"], writes=["vscr2"])
        s.add("act", lambda e, tt=tt: e.activation(
            vscr2[:], vscr2[:], AF.Ln, bias=eps_t[:, 0:1]),
            reads=["vscr2", "epsT"], writes=["vscr2"])
        s.add("act", lambda e, tt=tt, sl=sl: e.activation(
            rstdS[:, sl], vscr2[:], AF.Exp, scale=-0.5),
            reads=["vscr2"], writes=["rstd"])
    for c in range(NDC):
        s.add("dve", lambda e, c=c: e.tensor_sub(xf[c][:], xf[c][:], meanS[:]),
              reads=["xf%d" % c, "mean"], writes=["xf%d" % c])
    for c in range(NDC):
        s.add("dve", lambda e, c=c: e.tensor_mul(xn[c][:], xf[c][:], rstdS[:]),
              reads=["xf%d" % c, "rstd"], writes=["xn%d" % c])
    s.flush("ln")
    for g in reversed(g_lnps):
        g.__exit__(None, None, None)
    for g in reversed(g_more):
        g.__exit__(None, None, None)
    for g in reversed(g_xf):
        g.__exit__(None, None, None)

    if DEBUG_OUT and LAST_PHASE == "ln":
        _debug_dump(nc, s, dbg_out, {"xn": xn})
        nc.finalize()
        return nc

    g_gate = [nc.sbuf_tensor("g_%s%d" % (nm, i), [128, w],
                             BF16 if nm == "cb" else F32)
              for i in range(2)
              for (nm, w) in (("pr", 8), ("v", 8), ("sel", 8), ("cb", 8),
                              ("ex", 8), ("m1", 1), ("nm", 1), ("n1", 1),
                              ("m2", 1), ("tie", 1), ("den", 1), ("rd", 1))]
    gh_ = [g.__enter__() for g in g_gate]
    gt = [gh_[0:12], gh_[12:24]]
    GN = ("pr", "v", "sel", "cb", "ex", "m1", "nm", "n1", "m2", "tie", "den",
          "rd")
    gv = [dict(zip(GN, gt[i])) for i in range(2)]
    g_lg = [nc.psum_tensor("lgps%d" % i, [128, 8], F32) for i in range(2)]
    lgps = [g.__enter__() for g in g_lg]
    g_tr = [nc.psum_tensor("trps%d" % i, [8, 128], BF16) for i in range(2)]
    trps = [g.__enter__() for g in g_tr]

    def gate_chain(tc):
        p = tc % 2
        t = gv[p]
        P = "_%d" % p

        def A(eng, fn, reads, writes):
            return lambda: s.add(eng, fn, reads=reads, writes=writes)
        ops = []
        for cc in range(NDC):
            ops.append(A("pe", lambda e, tc=tc, cc=cc, p=p: e.matmul(
                lgps[p][:], xn[cc][:, tc * 128:(tc + 1) * 128],
                wgb[:, cc * 8:(cc + 1) * 8], start=(cc == 0),
                stop=(cc == NDC - 1)),
                ["xn%d" % cc, "wgb"], ["ps_lg" + P]))
        ops += [
            A("dve", lambda e, t=t, p=p: e.reduce_max(t["m1"][:], lgps[p][:],
                                                      axis=AX.X),
              ["ps_lg" + P], ["g_m1" + P]),
            A("dve", lambda e, t=t: e.tensor_scalar_mul(t["nm"][:], t["m1"][:],
                                                        -1.0),
              ["g_m1" + P], ["g_nm" + P]),
            A("act", lambda e, t=t, p=p: e.activation(
                t["pr"][:], lgps[p][:], AF.Exp, bias=t["nm"][:, 0:1]),
              ["ps_lg" + P, "g_nm" + P], ["g_pr" + P]),
            A("dve", lambda e, t=t: e.reduce_sum(t["m1"][:], t["pr"][:],
                                                 axis=AX.X),
              ["g_pr" + P], ["g_m1" + P]),
            A("dve", lambda e, t=t: e.reciprocal(t["rd"][:], t["m1"][:]),
              ["g_m1" + P], ["g_rd" + P]),
            A("dve", lambda e, t=t: e.tensor_scalar_mul(t["v"][:], t["pr"][:],
                                                        t["rd"][:, 0:1]),
              ["g_pr" + P, "g_rd" + P], ["g_v" + P]),
            A("dve", lambda e, t=t: e.reduce_max(t["m1"][:], t["v"][:],
                                                 axis=AX.X),
              ["g_v" + P], ["g_m1" + P]),
            A("dve", lambda e, t=t: e.tensor_single_scalar(
                t["sel"][:], t["v"][:], t["m1"][:, 0:1], ALU.is_ge),
              ["g_v" + P, "g_m1" + P], ["g_sel" + P]),
            A("dve", lambda e, t=t: e.reduce_sum(t["n1"][:], t["sel"][:],
                                                 axis=AX.X),
              ["g_sel" + P], ["g_n1" + P]),
            A("dve", lambda e, t=t: e.scalar_tensor_tensor(
                t["ex"][:], t["sel"][:], -1e9, t["v"][:], ALU.mult, ALU.add),
              ["g_sel" + P, "g_v" + P], ["g_ex" + P]),
            A("dve", lambda e, t=t: e.reduce_max(t["m2"][:], t["ex"][:],
                                                 axis=AX.X),
              ["g_ex" + P], ["g_m2" + P]),
            A("dve", lambda e, t=t: e.tensor_single_scalar(
                t["tie"][:], t["n1"][:], 1.5, ALU.is_ge),
              ["g_n1" + P], ["g_tie" + P]),
            A("dve", lambda e, t=t: e.tensor_sub(t["den"][:], t["m1"][:],
                                                 t["m2"][:]),
              ["g_m1" + P, "g_m2" + P], ["g_den" + P]),
            A("dve", lambda e, t=t: e.scalar_tensor_tensor(
                t["m2"][:], t["den"][:], t["tie"][:, 0:1], t["m2"][:],
                ALU.mult, ALU.add),
              ["g_den" + P, "g_tie" + P, "g_m2" + P], ["g_m2" + P]),
            A("dve", lambda e, t=t: e.tensor_scalar(
                t["den"][:], t["m1"][:], t["m2"][:, 0:1], 1e-9, ALU.add,
                ALU.add),
              ["g_m1" + P, "g_m2" + P], ["g_den" + P]),
            A("dve", lambda e, t=t: e.reciprocal(t["rd"][:], t["den"][:]),
              ["g_den" + P], ["g_rd" + P]),
            A("dve", lambda e, t=t: e.tensor_single_scalar(
                t["sel"][:], t["v"][:], t["m2"][:, 0:1], ALU.is_ge),
              ["g_v" + P, "g_m2" + P], ["g_sel" + P]),
            A("dve", lambda e, t=t: e.tensor_mul(t["v"][:], t["v"][:],
                                                 t["sel"][:]),
              ["g_v" + P, "g_sel" + P], ["g_v" + P]),
            A("dve", lambda e, t=t: e.tensor_scalar_mul(
                t["cb"][:], t["v"][:], t["rd"][:, 0:1]),
              ["g_v" + P, "g_rd" + P], ["g_cb" + P]),
            A("pe", lambda e, t=t, p=p: e.transpose(
                trps[p][:], t["cb"][:], ident),
              ["g_cb" + P, "consts"], ["ps_tr" + P]),
            A("dve", lambda e, tc=tc, p=p: e.tensor_copy(
                combT[:, tc * 128:(tc + 1) * 128], trps[p][:]),
              ["ps_tr" + P], ["combT"]),
        ]
        return ops

    for tcp in range(0, NTC, 2):
        a_ops = gate_chain(tcp)
        b_ops = gate_chain(tcp + 1)
        n = max(len(a_ops), len(b_ops))
        for i in range(n):
            if i < len(a_ops):
                a_ops[i]()
            if i < len(b_ops):
                b_ops[i]()
    s.flush("gate")
    for g in reversed(g_tr):
        g.__exit__(None, None, None)
    for g in reversed(g_lg):
        g.__exit__(None, None, None)
    for g in reversed(g_gate):
        g.__exit__(None, None, None)

    NPASS = 16
    WRN = 3
    g_m = (
        [nc.sbuf_tensor("qr%d" % i, [128, T], BF16) for i in range(2)] +
        [nc.sbuf_tensor("kr%d" % i, [128, T], BF16) for i in range(2)] +
        [nc.sbuf_tensor("tab%d" % i, [128, T], BF16) for i in range(4)] +
        [nc.sbuf_tensor("vaug%d" % i, [128, 2 * 16 * 65], BF16) for i in range(2)] +
        [nc.sbuf_tensor("Psb%d" % i, [128, 512], BF16) for i in range(4)] +
        [nc.sbuf_tensor("opair%d" % i, [128, T], BF16) for i in range(2)] +
        [nc.sbuf_tensor("wsf%d" % i, [128, 1024], F32) for i in range(3)] +
        [nc.sbuf_tensor("qkb%d" % i, [128, 1024], BF16) for i in range(2)] +
        [nc.sbuf_tensor("wvb", [128, 1024], BF16)] +
        [nc.sbuf_tensor("woutb%d" % i, [128, 1024], BF16) for i in range(2)] +
        [nc.sbuf_tensor("woq%d" % i, [128, 1024], BF16) for i in range(2)] +
        [nc.sbuf_tensor("w1b%d" % i, [128, 1024], BF16) for i in range(4)] +
        [nc.sbuf_tensor("cbs%d" % i, [128, T], BF16) for i in range(2)] +
        [nc.sbuf_tensor("asf%d" % i, [128, 1024], BF16) for i in range(2)] +
        [nc.sbuf_tensor("sgf%d" % i, [128, 1024], BF16) for i in range(2)] +
        [nc.sbuf_tensor("tsf%d" % i, [128, 1024], BF16) for i in range(2)] +
        [nc.sbuf_tensor("b1h", [128, 256], F32),
         nc.sbuf_tensor("rss", [1, 512], F32),
         nc.sbuf_tensor("gsum", [128, T], F32),
         nc.sbuf_tensor("gsb0", [128, T], BF16),
         nc.sbuf_tensor("gsb1", [128, T], BF16),
         nc.sbuf_tensor("ksqB0", [64, 512], BF16),
         nc.sbuf_tensor("ksqB1", [64, 512], BF16),
         nc.sbuf_tensor("nsc", [128, 8], F32),
         nc.sbuf_tensor("nln", [128, 2], F32),
         nc.sbuf_tensor("ropet", [128, 512], BF16)]
    )
    hm = [g.__enter__() for g in g_m]
    qr, kr, tabs, vaug = hm[0:2], hm[2:4], hm[4:8], hm[8:10]
    Psb, opair, wsf = hm[10:14], hm[14:16], hm[16:19]
    qkb, wvb = hm[19:21], hm[21]
    woutb, woqx, w1b = hm[22:24], hm[24:26], hm[26:30]
    cbs, asf, sgf, tsf = hm[30:32], hm[32:34], hm[34:36], hm[36:38]
    (b1h, rs_s, gsum, gsb0, gsb1, ksqB0, ksqB1, nsc, nln, t1) = hm[38:48]
    ksqX = [ksqB0, ksqB1]
    gsbx = [gsb0, gsb1]

    g_ps = [nc.psum_tensor("sps%d" % i, [128, 512], F32) for i in range(3)] + \
           [nc.psum_tensor("ops%d" % i, [65, 512], F32) for i in range(2)] + \
           [nc.psum_tensor("pjps", [128, 512], F32),
            nc.psum_tensor("hpa", [128, 512], F32),
            nc.psum_tensor("hpb", [128, 512], F32)]
    (sps0, sps1, sps2, ops0, ops1, pjps, hpa, hpb) = [g.__enter__() for g in g_ps]
    sps = [sps0, sps1, sps2]
    opsb = [ops0, ops1]
    hpx = [hpa, hpb]
    vwps = pjps
    ones64x128 = consts[0:64, C_ONE:C_ONE + 128]

    for i in range(4):
        s.dma(lambda e, i=i: e.dma_start(tabs[i][:], d_tab[i]),
              writes=["tab%d" % i])
    for dc in range(NDC):
        s.add("pool", lambda e, dc=dc: e.memset(acc[dc][:], 0.0),
              writes=["acc%d" % dc])
    s.add("dve", lambda e: e.tensor_scalar_mul(b1h[:], b1_all[:], 0.5),
          reads=["b1all"], writes=["b1h"])

    wring = [0]

    def wload(dst_ap, dst_tok, src_ap, tag, eng="pool"):
        i = wring[0] % WRN
        wring[0] += 1
        s.dma(lambda e, i=i, src_ap=src_ap: e.dma_start(wsf[i][:], src_ap),
              writes=["wsf%d" % i], tag=tag)
        if eng == "act":
            s.add("act", lambda e, i=i, dst_ap=dst_ap: e.activation(
                dst_ap, wsf[i][:], AF.Identity), reads=["wsf%d" % i],
                writes=[dst_tok])
        else:
            s.add("pool", lambda e, i=i, dst_ap=dst_ap: e.tensor_copy(
                dst_ap, wsf[i][:]), reads=["wsf%d" % i], writes=[dst_tok])

    pend = []

    def flush_pend(n=None):
        todo = pend[:n] if n else pend[:]
        del pend[:len(todo)]
        for fn in todo:
            fn()

    def gen_ffn():
        gh = [0]
        wload(woqx[0][:], "woq0", d_wf[0], "woqL0")
        for ex in range(E):
            cbp = ex % 2
            for tt in range(NTT):
                sl = slice(tt * 512, (tt + 1) * 512)
                cbk = tt % 2
                s.add("pe", lambda e, ex=ex, sl=sl, cbk=cbk: e.matmul(
                    hpx[cbk][:], sel8[:, ex * 128:(ex + 1) * 128],
                    combT[0:8, sl], start=True, stop=True),
                    reads=["combT", "consts"], writes=["ps_hp%d" % cbk])
                yield
                s.add("act", lambda e, cbp=cbp, sl=sl, cbk=cbk: e.activation(
                    cbs[cbp][:, sl], hpx[cbk][:], AF.Identity, scale=0.5),
                    reads=["ps_hp%d" % cbk], writes=["cbs%d" % cbp])
            s.dma(lambda e, ex=ex, cbp=cbp: e.dma_start(d_cbs[ex], cbs[cbp][:]),
                  reads=["cbs%d" % cbp], writes=["cbsout%d" % ex], tag="cbo%d" % ex)
        for p in range(NPASS):
            for ex in range(E):
                cbp = ex % 2
                s.dma(lambda e, ex=ex, cbp=cbp: e.dma_start(
                    cbs[cbp][:], d_cbs[ex]),
                    reads=["cbsout%d" % ex], writes=["cbs%d" % cbp],
                    tag="cbi_%d_%d" % (p, ex))
                ia = (2 * (p * E + ex)) % 4
                ib = ia + 1
                wload(w1b[ia][:], "w1b%d" % ia, d_w1[ex, p], "w1a_%d_%d" % (ex, p))
                wload(w1b[ib][:], "w1b%d" % ib, d_w1[ex, 16 + p],
                      "w1b_%d_%d" % (ex, p))
                b1a = b1_all[:, ex * 32 + p:ex * 32 + p + 1]
                b1b = b1h[:, ex * 32 + 16 + p:ex * 32 + 16 + p + 1]
                for hf in range(2):
                    hpar = gh[0] % 2
                    gh[0] += 1
                    for tq in range(2):
                        tt = 2 * hf + tq
                        sl = slice(tt * 512, (tt + 1) * 512)
                        qsl = slice(tq * 512, (tq + 1) * 512)
                        for cc in range(NDC):
                            s.add("pe", lambda e, ia=ia, cc=cc, sl=sl: e.matmul(
                                hpa[:], w1b[ia][:, cc * 128:(cc + 1) * 128],
                                xn[cc][:, sl], start=(cc == 0), stop=(cc == NDC - 1)),
                                reads=["w1b%d" % ia, "xn%d" % cc], writes=["ps_hp0"])
                            yield
                        s.add("dve", lambda e, sl=sl, qsl=qsl, hpar=hpar, b=b1a,
                              cbp=cbp: e.scalar_tensor_tensor(
                                  asf[hpar][:, qsl], hpa[:], b, cbs[cbp][:, sl],
                                  ALU.add, ALU.mult),
                              reads=["ps_hp0", "b1all", "cbs%d" % cbp],
                              writes=["as%d_%d" % (hpar, tq)])
                        flush_pend(1)
                        for cc in range(NDC):
                            s.add("pe", lambda e, ib=ib, cc=cc, sl=sl: e.matmul(
                                hpb[:], w1b[ib][:, cc * 128:(cc + 1) * 128],
                                xn[cc][:, sl], start=(cc == 0), stop=(cc == NDC - 1)),
                                reads=["w1b%d" % ib, "xn%d" % cc], writes=["ps_hp1"])
                            yield
                        s.add("act", lambda e, qsl=qsl, hpar=hpar, b=b1b:
                              e.activation(sgf[hpar][:, qsl], hpb[:], AF.Tanh,
                                           bias=b, scale=0.5),
                              reads=["ps_hp1", "b1h"],
                              writes=["sg%d_%d" % (hpar, tq)])
                        flush_pend(1)
                    hsl = slice(hf * 1024, (hf + 1) * 1024)
                    rd = ["as%d_%d" % (hpar, q) for q in range(2)] + \
                         ["sg%d_%d" % (hpar, q) for q in range(2)]
                    if ex == 0:
                        pend.append(lambda hsl=hsl, hpar=hpar, rd=rd, hf=hf: s.add(
                            "dve", lambda e: e.scalar_tensor_tensor(
                                gsum[:, hsl], sgf[hpar][:], 1.0, asf[hpar][:],
                                ALU.add, ALU.mult),
                            reads=rd, writes=["gs%d" % hf]))
                    elif ex < E - 1:
                        pend.append(lambda hpar=hpar, rd=rd: s.add(
                            "dve", lambda e: e.scalar_tensor_tensor(
                                tsf[hpar][:], sgf[hpar][:], 1.0, asf[hpar][:],
                                ALU.add, ALU.mult),
                            reads=rd, writes=["ts%d" % hpar]))
                        pend.append(lambda hsl=hsl, hpar=hpar, hf=hf: s.add(
                            "dve", lambda e: e.tensor_add(
                                gsum[:, hsl], tsf[hpar][:], gsum[:, hsl]),
                            reads=["ts%d" % hpar, "gs%d" % hf],
                            writes=["gs%d" % hf]))
                    else:
                        pend.append(lambda hpar=hpar, rd=rd: s.add(
                            "dve", lambda e: e.scalar_tensor_tensor(
                                tsf[hpar][:], sgf[hpar][:], 1.0, asf[hpar][:],
                                ALU.add, ALU.mult),
                            reads=rd, writes=["ts%d" % hpar]))
                        gx = p % 2
                        pend.append(lambda hsl=hsl, hpar=hpar, hf=hf, gx=gx: s.add(
                            "dve", lambda e: e.tensor_add(
                                gsbx[gx][:, hsl], tsf[hpar][:], gsum[:, hsl]),
                            reads=["ts%d" % hpar, "gs%d" % hf],
                            writes=["gq%d_%d" % (gx, hf)]))
            flush_pend()
            if p % 2 == 1:
                last = (p == NPASS - 1)
                for tt in range(NTT):
                    for dc in range(NDC):
                        sl = slice(tt * 512, (tt + 1) * 512)
                        ep = dc % 2
                        for gx in range(2):
                            s.add("pe", lambda e, dc=dc, sl=sl, ep=ep, gx=gx,
                                  last=last: e.matmul(
                                      hpx[ep][:],
                                      woqx[gx][:, dc * 128:(dc + 1) * 128],
                                      gsbx[gx][:, sl], start=(gx == 0),
                                      stop=(gx == 1 and not last)),
                                  reads=["woq%d" % gx, "gq%d_%d" % (gx, tt // 2)],
                                  writes=["ps_hp%d" % ep])
                            yield
                        if last:
                            s.add("pe", lambda e, dc=dc, sl=sl, ep=ep: e.matmul(
                                hpx[ep][:], boRep[0:8, dc * 128:(dc + 1) * 128],
                                combT[0:8, sl], start=False, stop=True),
                                reads=["boRep", "combT"], writes=["ps_hp%d" % ep])
                            yield
                        s.add("dve", lambda e, dc=dc, sl=sl, ep=ep: e.tensor_add(
                            acc[dc][:, sl], hpx[ep][:], acc[dc][:, sl]),
                            reads=["ps_hp%d" % ep, "acc%d" % dc],
                            writes=["acc%d" % dc])
            if p < NPASS - 1:
                wload(woqx[(p + 1) % 2][:], "woq%d" % ((p + 1) % 2),
                      d_wf[p + 1], "woq%d" % (p + 1))

    genf = gen_ffn()

    _DONE = object()

    def fill(n):
        for _ in range(n):
            if next(genf, _DONE) is _DONE:
                break

    def prologue_loads(c):
        wload(qkb[0][:], "qkb0", d_qk[c], "wq%d" % c)
        wload(qkb[1][:], "qkb1", d_qk[8 + c], "wk%d" % c)
        wload(wvb[:], "wvb", d_wv[c], "wv%d" % c)

    def prologue(c):
        r = c % 2
        for (wi, dst, dtok) in ((0, qr[r], "qr%d" % r), (1, kr[r], "kr%d" % r)):
            for tt in range(NTT):
                sl = slice(tt * 512, (tt + 1) * 512)
                fill(1)
                for cc in range(NDC):
                    s.add("pe", lambda e, wi=wi, cc=cc, sl=sl: e.matmul(
                        pjps[:], qkb[wi][:, cc * 128:(cc + 1) * 128],
                        xn[cc][:, sl], start=(cc == 0), stop=(cc == NDC - 1)),
                        reads=["qkb%d" % wi, "xn%d" % cc], writes=["ps_pj"])
                fill(1)
                s.add("act", lambda e, dst=dst, sl=sl: e.activation(
                    dst[:, sl], pjps[:], AF.Identity),
                    reads=["ps_pj"], writes=[dtok])
                fill(1)
        for (dst, dtok, tc_, ts_) in ((qr[r], "qr%d" % r, 0, 1),
                                      (kr[r], "kr%d" % r, 2, 3)):
            for tt in range(NTT):
                sl = slice(tt * 512, (tt + 1) * 512)
                s.add("pe", lambda e, dst=dst, sl=sl: e.matmul(
                    pjps[:], rotT, dst[:, sl], start=True, stop=True),
                    reads=[dtok, "consts"], writes=["ps_pj"])
                fill(1)
                s.add("dve", lambda e, dst=dst, sl=sl, tc_=tc_:
                      e.tensor_mul(t1[:], dst[:, sl], tabs[tc_][:, sl]),
                      reads=[dtok, "tab%d" % tc_], writes=["ropet"])
                s.add("dve", lambda e, sl=sl, ts_=ts_:
                      e.tensor_mul(pjps[:], pjps[:], tabs[ts_][:, sl]),
                      reads=["ps_pj", "tab%d" % ts_], writes=["ps_pj"])
                s.add("dve", lambda e, dst=dst, sl=sl: e.tensor_add(
                    dst[:, sl], t1[:], pjps[:]),
                    reads=["ropet", "ps_pj"], writes=[dtok])
                fill(1)
        for hh in (0, 1):
            h = 2 * c + hh
            hl = hh * 64
            for (stile, stok, col0) in ((qr[r], "qr%d" % r, 0),
                                        (kr[r], "kr%d" % r, 4)):
                for lt in range(NTT):
                    sl = slice(lt * 512, (lt + 1) * 512)
                    kb = lt % 2
                    s.add("act", lambda e, stile=stile, hl=hl, sl=sl, kb=kb:
                          e.activation(
                              ksqX[kb][:], stile[hl:hl + 64, sl], AF.Square),
                          reads=[stok], writes=["ksqB%d" % kb])
                    s.add("pe", lambda e, kb=kb: e.matmul(
                        pjps[:], ones64x128, ksqX[kb][:], start=True, stop=True),
                        reads=["ksqB%d" % kb, "consts"], writes=["ps_pj"])
                    fill(1)
                    s.add("dve", lambda e, col=col0 + lt: e.reduce_max(
                        nsc[:, col:col + 1], pjps[:], axis=AX.X),
                        reads=["ps_pj"], writes=["nsc%d" % (col0 + lt)])
            s.add("dve", lambda e: e.reduce_max(nln[:, 0:1], nsc[:, 0:4], axis=AX.X),
                  reads=["nsc0", "nsc1", "nsc2", "nsc3"], writes=["nl0"])
            s.add("dve", lambda e: e.reduce_max(nln[:, 1:2], nsc[:, 4:8], axis=AX.X),
                  reads=["nsc4", "nsc5", "nsc6", "nsc7"], writes=["nl1"])
            s.add("dve", lambda e: e.tensor_scalar_mul(
                nln[:, 0:1], nln[:, 0:1], -4.0),
                reads=["nl0"], writes=["nl0"])
            s.add("dve", lambda e, h=h: e.scalar_tensor_tensor(
                bias16[:, h:h + 1], nln[:, 1:2], -0.0625, nln[:, 0:1],
                ALU.mult, ALU.add),
                reads=["nl0", "nl1"], writes=["bias%d" % h])
        ones_ap = vaug[r][:, :].rearrange("p (g t s) -> p g t s", g=2, t=16,
                                          s=65)[:, :, :, 64:65]
        s.add("pool", lambda e, ones_ap=ones_ap: e.memset(ones_ap, 1.0),
              writes=["vaug%d" % r])
        for tc in range(NTC):
            for cc in range(NDC):
                s.add("pe", lambda e, tc=tc, cc=cc: e.matmul(
                    sps[2][:, 0:128], xn[cc][:, tc * 128:(tc + 1) * 128],
                    wvb[:, cc * 128:(cc + 1) * 128],
                    start=(cc == 0), stop=(cc == NDC - 1)),
                    reads=["wvb", "xn%d" % cc], writes=["ps_s2"])
            fill(3)
            dstv = vaug[r][:, :].rearrange("p (g rr) -> p g rr", g=2, rr=1040)[
                :, :, tc * 65:tc * 65 + 64]
            srcv = sps[2][:, 0:128].rearrange("p (g x) -> p g x", g=2, x=64)
            s.add("act", lambda e, dstv=dstv, srcv=srcv: e.activation(
                dstv, srcv, AF.Identity),
                reads=["ps_s2"], writes=["vaug%d" % r])
            fill(1)

    apend = []

    def flush_anorm():
        while apend:
            apend.pop(0)[1]()

    def FILLQ(mc):
        fill(6)

    def head_lt(h, lt):
        c, hl = h // 2, (h % 2) * 64
        r, vp, hg = c % 2, c % 2, h % 2
        qtok, ktok = "qr%d" % r, "kr%d" % r
        sl = slice(lt * 512, (lt + 1) * 512)
        ob = lt % 2

        def _smm(mc):
            sb = mc % 4
            spb = sps[sb] if sb < 3 else pjps
            stok = ("ps_s%d" % sb) if sb < 3 else "ps_pj"
            s.add("pe", lambda e, mc=mc, spb=spb, stok=stok: e.matmul(
                spb[:], kr[r][hl:hl + 64, mc * 128:(mc + 1) * 128],
                qr[r][hl:hl + 64, sl], start=True, stop=True),
                reads=[ktok, qtok], writes=[stok])
            s.add("act", lambda e, sb=sb, spb=spb, stok=stok: e.activation(
                Psb[sb][:], spb[:], AF.Exp, bias=bias16[:, h:h + 1]),
                reads=[stok, "bias%d" % h], writes=["Psb%d" % sb])

        _smm(0)
        _smm(1)
        _smm(2)
        _smm(3)
        for mc in range(NTC):
            pb = mc % 4
            FILLQ(mc)
            s.add("pe", lambda e, mc=mc, pb=pb: e.matmul(
                opsb[ob][:],
                vaug[vp][:, hg * 1040 + mc * 65:hg * 1040 + mc * 65 + 65],
                Psb[pb][:], start=(mc == 0), stop=(mc == NTC - 1)),
                reads=["vaug%d" % vp, "Psb%d" % pb], writes=["ps_o%d" % ob])
            if mc + 4 < NTC:
                _smm(mc + 4)
            if mc == 1:
                while apend and apend[0][0] == 0:
                    apend.pop(0)[1]()
            if mc == 14:
                flush_anorm()
        apend.append((0, lambda ob=ob: s.add(
            "dve", lambda e: e.reciprocal(rs_f[:], opsb[ob][64:65, :]),
            reads=["ps_o%d" % ob], writes=["rsf"])))
        apend.append((1, lambda: s.add(
            "act", lambda e: e.activation(rs_b[:], rs_f[:], AF.Identity),
            reads=["rsf"], writes=["rsb"])))
        apend.append((1, lambda: s.add(
            "pe", lambda e: e.matmul(pjps[0:64, :], onesrow64, rs_b[:],
                                     start=True, stop=True),
            reads=["rsb", "consts"], writes=["ps_pj"])))
        apend.append((1, lambda: s.add(
            "act", lambda e: e.activation(bc_sb[:], pjps[0:64, :], AF.Identity),
            reads=["ps_pj"], writes=["bcsb"])))
        apend.append((1, lambda c=c, hl=hl, sl=sl, ob=ob: s.add(
            "dve", lambda e: e.tensor_mul(
                opair[c % 2][hl:hl + 64, sl], opsb[ob][0:64, :], bc_sb[:]),
            reads=["ps_o%d" % ob, "bcsb"], writes=["opair%d" % (c % 2)])))

    def outproj(c):
        flush_anorm()
        for tt in range(NTT):
            for dc in range(NDC):
                sl = slice(tt * 512, (tt + 1) * 512)
                op_ = dc % 2
                s.add("pe", lambda e, c=c, dc=dc, sl=sl, op_=op_: e.matmul(
                    sps[op_][:], woutb[c % 2][:, dc * 128:(dc + 1) * 128],
                    opair[c % 2][:, sl], start=True, stop=True),
                    reads=["woutb%d" % (c % 2), "opair%d" % (c % 2)],
                    writes=["ps_s%d" % op_])
                fill(2)
                s.add("dve", lambda e, dc=dc, sl=sl, op_=op_: e.tensor_add(
                    acc[dc][:, sl], sps[op_][:], acc[dc][:, sl]),
                    reads=["ps_s%d" % op_, "acc%d" % dc], writes=["acc%d" % dc])
                fill(1)

    prologue_loads(0)
    prologue(0)
    for c in range(NDC):
        wload(woutb[c % 2][:], "woutb%d" % (c % 2), d_wo[c], "wout%d" % c)
        if c + 1 < NDC:
            prologue_loads(c + 1)
        for lt in range(NTT):
            head_lt(2 * c, lt)
        if c + 1 < NDC:
            prologue(c + 1)
        for lt in range(NTT):
            head_lt(2 * c + 1, lt)
        outproj(c)
    for _ in genf:
        pass
    flush_pend()
    s.flush("merged")
    for g in reversed(g_ps):
        g.__exit__(None, None, None)
    for g in reversed(g_m):
        g.__exit__(None, None, None)

    if DEBUG_OUT and LAST_PHASE in ("qkv", "attn", "ffn"):
        _debug_dump(nc, s, dbg_out, {"acc": acc})
        nc.finalize()
        return nc


    g_fin = [nc.sbuf_tensor("fxf%d" % i, [128, T], F32) for i in range(4)] + \
            [nc.sbuf_tensor("fout%d" % i, [128, T], F32) for i in range(4)]
    fxf = [g.__enter__() for g in g_fin[0:4]]
    fout = [g.__enter__() for g in g_fin[4:8]]
    for dc in range(NDC):
        p = dc % 4
        s.dma(lambda e, dc=dc, p=p: e.dma_start(
            fxf[p][:], d_xT[dc * 128:(dc + 1) * 128, :]),
            reads=["fout%d" % p], writes=["fxf%d" % p])
        s.add("dve", lambda e, dc=dc, p=p: e.tensor_add(
            fout[p][:], fxf[p][:], acc[dc][:]),
            reads=["fxf%d" % p, "acc%d" % dc], writes=["fout%d" % p])
        s.dma(lambda e, dc=dc, p=p: e.dma_start(
            d_out[dc * 128:(dc + 1) * 128, :], fout[p][:]),
            reads=["fout%d" % p], writes=["out%d" % dc], eng="pool")
    s.add("pool", lambda e: e.nop(),
          reads=["out%d" % dc for dc in range(NDC)], tag="fence")
    s.flush("final")
    for g in reversed(g_fin):
        g.__exit__(None, None, None)

    nc.finalize()
    return nc


def _emit_vpair(s, nc, pair, d_wv, wvf, wvb, vaug, xn, vps):
    p = pair % 2
    s.dma(lambda e, pair=pair, p=p: e.dma_start(wvf[p][:], d_wv[pair]),
          reads=["wvf%d" % p], writes=["wvf%d" % p], tag="wv%d" % pair)
    s.add("pool", lambda e, p=p: e.tensor_copy(wvb[p][:], wvf[p][:]),
          reads=["wvf%d" % p], writes=["wvb%d" % p])
    ones_ap = vaug[p][:, :].rearrange("p (g t s) -> p g t s", g=2, t=16, s=65)[
        :, :, :, 64:65]
    s.add("pool", lambda e, ones_ap=ones_ap: e.memset(ones_ap, 1.0),
          writes=["vaug%d" % p])
    for tc in range(NTC):
        for c in range(NDC):
            s.add("pe", lambda e, p=p, tc=tc, c=c: e.matmul(
                vps[:], xn[c][:, tc * 128:(tc + 1) * 128],
                wvb[p][:, c * 128:(c + 1) * 128],
                start=(c == 0), stop=(c == NDC - 1)),
                reads=["wvb%d" % p, "xn%d" % c], writes=["ps_v"])
        dst = vaug[p][:, :].rearrange("p (g r) -> p g r", g=2, r=1040)[
            :, :, tc * 65:tc * 65 + 64]
        src = vps[:].rearrange("p (g x) -> p g x", g=2, x=64)
        s.add("act", lambda e, dst=dst, src=src: e.activation(
            dst, src, AF.Identity),
            reads=["ps_v"], writes=["vaug%d" % p])


def _debug_dump(nc, s, dbg_out, tensors):
    row = 0
    guards = [nc.sbuf_tensor("dbgcvt", [128, 2560], F32)]
    cvt = guards[0].__enter__()
    for name, tiles in tensors.items():
        for t in tiles:
            npart = t[:].shape[0]
            fw = t[:].shape[-1]
            s.add("dve", lambda e, t=t, npart=npart, fw=fw: e.tensor_copy(
                cvt[0:npart, 0:fw], t[:]), reads=["dbgcvt"], writes=["dbgcvt"])
            s.dma(lambda e, row=row, npart=npart, fw=fw: e.dma_start(
                dbg_out[row:row + npart, 0:fw], cvt[0:npart, 0:fw]),
                reads=["dbgcvt"], writes=["dbg"])
            row += npart
    s.add("pool", lambda e: e.nop(), reads=["dbg"], tag="fence")
    s.flush("debug")
    for g in reversed(guards):
        g.__exit__(None, None, None)



def _host_prep(L0_attn_x, L0_attn_w_qkv, L0_attn_w_out, L0_ffn_Wg, L0_ffn_W1,
               L0_ffn_b1, L0_ffn_Wo, L0_ffn_bo):
    x = np.asarray(L0_attn_x, dtype=np.float32)
    wqkv = np.asarray(L0_attn_w_qkv, dtype=np.float32)
    wout = np.asarray(L0_attn_w_out, dtype=np.float32)
    Wg = np.asarray(L0_ffn_Wg, dtype=np.float32)
    W1 = np.asarray(L0_ffn_W1, dtype=np.float32)
    b1 = np.asarray(L0_ffn_b1, dtype=np.float32)
    Wo = np.asarray(L0_ffn_Wo, dtype=np.float32)
    bo = np.asarray(L0_ffn_bo, dtype=np.float32)

    xT = np.ascontiguousarray(x.transpose(0, 2, 1))
    WT = np.ascontiguousarray(wqkv.T)
    wqkS = np.ascontiguousarray(
        WT[:, :2048].reshape(8, 128, 16, 128).transpose(2, 1, 0, 3)
    ).reshape(16, 128, 1024)
    wvS = np.ascontiguousarray(
        WT[:, 2048:].reshape(8, 128, 8, 128).transpose(2, 1, 0, 3)
    ).reshape(8, 128, 1024)
    woutS = np.ascontiguousarray(
        wout.T.reshape(8, 128, 8, 128)).reshape(8, 128, 1024)
    wgS = np.ascontiguousarray(Wg.T.reshape(8, 128, 8).transpose(1, 0, 2)
                               ).reshape(128, 64)
    w1S = np.ascontiguousarray(
        W1.transpose(0, 2, 1).reshape(E, 8, 128, 32, 128).transpose(0, 3, 2, 1, 4)
    ).reshape(E, 32, 128, 1024)
    wfS = np.ascontiguousarray(Wo.T.reshape(16, 128, 8, 128)).reshape(16, 128, 1024)
    b1S = np.ascontiguousarray(b1.reshape(8, 32, 128).transpose(2, 0, 1)
                               ).reshape(128, 256)
    boS = np.ascontiguousarray(np.broadcast_to(bo, (8, 1024)))

    consts = np.zeros((128, 1410), dtype=np.float32)
    consts[:, 0:128] = np.eye(128)
    R = np.zeros((64, 64), dtype=np.float32)
    for j in range(32):
        R[j, j + 32] = -1.0
        R[j + 32, j] = 1.0
    R2 = np.zeros((128, 128), dtype=np.float32)
    R2[0:64, 0:64] = R
    R2[64:128, 64:128] = R
    consts[:, 128:256] = R2.T
    consts[:, 256:384] = 1.0
    consts[0:64, 384] = 1.0
    consts[64:128, 385] = 1.0
    for ex in range(8):
        consts[ex, 386 + ex * 128:386 + (ex + 1) * 128] = 1.0
    constS = consts.astype(BF)

    half = 32
    inv = 1.0 / (10000.0 ** (np.arange(half, dtype=np.float64) / half))
    ang = np.arange(T, dtype=np.float64)[:, None] * inv[None, :]
    cos = np.cos(ang)
    sin = np.sin(ang)
    cos64 = np.concatenate([cos, cos], axis=1).T
    sin64 = np.concatenate([sin, sin], axis=1).T
    cos128 = np.concatenate([cos64, cos64], axis=0)
    sin128 = np.concatenate([sin64, sin64], axis=0)
    tabS = np.stack([
        cos128 * 0.125, sin128 * 0.125, cos128, sin128
    ]).astype(BF)

    shared = {"wqkS": wqkS, "wvS": wvS, "woutS": woutS, "wgS": wgS,
              "w1S": w1S, "wfS": wfS, "b1S": b1S, "boS": boS,
              "constS": constS, "tabS": tabS}
    in_maps = [dict(shared, xT=np.ascontiguousarray(xT[c])) for c in range(8)]
    return in_maps


def kernel(**inputs):
    in_maps = _host_prep(**inputs)
    nc = build_program()
    res = run_bass_kernel_spmd(nc, in_maps, core_ids=list(range(8)))
    outs = []
    for c in range(8):
        key = "dbgT" if DEBUG_OUT else "outT"
        o = res.results[c][key]
        outs.append(o if DEBUG_OUT else np.ascontiguousarray(o.T))
    return np.stack(outs).astype(np.float32)


# version 51
# speedup vs baseline: 1.0273x; 1.0273x over previous
import numpy as np
import ml_dtypes
import concourse.bass as bass
import concourse.mybir as mybir
from concourse.bass_utils import run_bass_kernel_spmd

F32 = mybir.dt.float32
BF16 = mybir.dt.bfloat16
AF = mybir.ActivationFunctionType
ALU = mybir.AluOpType
AX = mybir.AxisListType
BF = ml_dtypes.bfloat16

T = 2048
D = 1024
NH = 16
DH = 64
E = 8
HF = 4096
NTT = 4
NDC = 8
NTC = 16
NJ = 32

PHASES = ("ln", "gate", "qkv", "attn", "ffn", "final")
LAST_PHASE = "final"
DEBUG_OUT = None


ENGINES = ("sync", "pe", "act", "dve", "pool")


SAFE_DIST = 2


class _Op:
    __slots__ = ("eng", "fn", "deps", "marks", "idx", "inc_val", "tag", "drain",
                 "sem")

    def __init__(self, eng, fn, tag):
        self.eng = eng
        self.fn = fn
        self.deps = set()
        self.marks = []
        self.idx = -1
        self.inc_val = 0
        self.tag = tag
        self.drain = False
        self.sem = None


class Sched:
    def __init__(self, nc):
        self.nc = nc
        self.sem = {e: nc.alloc_semaphore("sched_" + e) for e in ENGINES}
        self.count = {e: 0 for e in ENGINES}
        self.dma_sems = {}
        self.ops = []
        self.writer = {}
        self.readers = {}
        self.nblk = 0

    def _dep(self, op, item):
        if isinstance(item, _Op):
            if item.eng == op.eng:
                op.deps.add(item)
            else:
                op.deps.add(item)
        else:
            op.marks.append(item)

    def add(self, eng, fn, reads=(), writes=(), tag=""):
        op = _Op(eng, fn, tag)
        for t in reads:
            w = self.writer.get(t)
            if w is not None:
                self._dep(op, w)
            self.readers.setdefault(t, []).append(op)
        for t in writes:
            w = self.writer.get(t)
            if w is not None:
                self._dep(op, w)
            for r in self.readers.get(t, ()):
                if r is not op:
                    self._dep(op, r)
            self.writer[t] = op
            self.readers[t] = []
        self.ops.append(op)
        return op

    def dma(self, fn, reads=(), writes=(), tag="", eng="sync"):
        op = self.add(eng, fn, reads, writes, tag)
        key = writes[0] if writes else tag
        if key not in self.dma_sems:
            self.dma_sems[key] = [self.nc.alloc_semaphore("dma_" + key), 0]
        op.sem = self.dma_sems[key]
        return op

    def flush(self, name):
        ops = self.ops
        per = {e: [] for e in ENGINES}
        for op in ops:
            per[op.eng].append(op)
            op.idx = len(per[op.eng]) - 1
        need = set()
        for op in ops:
            for d in op.deps:
                if isinstance(d, _Op) and d.eng != op.eng:
                    need.add(d)
        for e in ENGINES:
            for op in per[e]:
                if op.sem is not None:
                    op.sem[1] += 16
                    op.inc_val = op.sem[1]
                elif op in need:
                    self.count[e] += 1
                    op.inc_val = self.count[e]
        for e in ("act", "dve", "pool"):
            drained = -1
            for op in per[e]:
                for d in op.deps:
                    if d.eng == e and d.idx > drained and op.idx - d.idx < SAFE_DIST:
                        op.drain = True
                        drained = op.idx - 1
                        break
        blk = self.nc.Block("blk%d_%s" % (self.nblk, name))
        self.nblk += 1
        sched = self

        def emit(eng_name):
            def body(eng):
                seen = {}
                for op in per[eng_name]:
                    waits = {}
                    for d in op.deps:
                        if d.eng != eng_name and d.inc_val:
                            s = d.sem[0] if d.sem is not None else sched.sem[d.eng]
                            waits[s] = max(waits.get(s, 0), d.inc_val)
                    for (s, v) in op.marks:
                        waits[s] = max(waits.get(s, 0), v)
                    for s, v in waits.items():
                        if seen.get(s, -1) >= v:
                            continue
                        seen[s] = v
                        eng.wait_ge(s, v)
                    if op.drain:
                        eng.drain()
                    instr = op.fn(eng)
                    if op.inc_val:
                        if op.sem is not None:
                            instr.then_inc(op.sem[0], 16)
                        else:
                            instr.then_inc(sched.sem[eng_name], 1)
            return body

        with blk:
            if per["sync"]:
                blk.sync(emit("sync"))
            if per["pe"]:
                blk.tensor(emit("pe"))
            if per["act"]:
                blk.scalar(emit("act"))
            if per["dve"]:
                blk.vector(emit("dve"))
            if per["pool"]:
                blk.gpsimd(emit("pool"))
        neww, newr = {}, {}
        for t, w in self.writer.items():
            if isinstance(w, _Op):
                if w.eng == "sync" and w.inc_val:
                    neww[t] = (w.sem[0], w.inc_val)
            else:
                neww[t] = w
        self.writer = neww
        self.readers = newr
        self.ops = []



def build_program():
    nc = bass.Bass("TRN2", target_bir_lowering=False)
    s = Sched(nc)

    d_xT = nc.dram_tensor("xT", [D, T], F32, kind="ExternalInput").ap()
    d_qk = nc.dram_tensor("wqkS", [16, 128, 1024], F32, kind="ExternalInput").ap()
    d_wv = nc.dram_tensor("wvS", [8, 128, 1024], F32, kind="ExternalInput").ap()
    d_wo = nc.dram_tensor("woutS", [8, 128, 1024], F32, kind="ExternalInput").ap()
    d_wg = nc.dram_tensor("wgS", [128, 64], F32, kind="ExternalInput").ap()
    d_w1 = nc.dram_tensor("w1S", [E, NJ, 128, 1024], F32, kind="ExternalInput").ap()
    d_wf = nc.dram_tensor("wfS", [16, 128, 1024], F32, kind="ExternalInput").ap()
    d_b1 = nc.dram_tensor("b1S", [128, 256], F32, kind="ExternalInput").ap()
    d_bo = nc.dram_tensor("boS", [8, 1024], F32, kind="ExternalInput").ap()
    d_cn = nc.dram_tensor("constS", [128, 1410], BF16, kind="ExternalInput").ap()
    d_tab = nc.dram_tensor("tabS", [4, 128, T], BF16, kind="ExternalInput").ap()
    d_cbs = nc.dram_tensor("cbsScratch", [E, 128, T], BF16, kind="Internal").ap()
    d_out = nc.dram_tensor("outT", [D, T], F32, kind="ExternalOutput").ap()
    dbg_out = None
    if DEBUG_OUT:
        dbg_out = nc.dram_tensor("dbgT", [4096, 2560], F32, kind="ExternalOutput").ap()

    C_I, C_R, C_ONE, C_HS, C_SEL = 0, 128, 256, 384, 386

    xn = [nc.alloc_sbuf_tensor("xn%d" % c, [128, T], BF16) for c in range(NDC)]
    combT = nc.alloc_sbuf_tensor("combT", [8, T], BF16)
    b1_all = nc.alloc_sbuf_tensor("b1all", [128, 256], F32)
    boRep = nc.alloc_sbuf_tensor("boRep", [8, 1024], BF16)
    consts = nc.alloc_sbuf_tensor("consts", [128, 1410], BF16)
    acc = [nc.alloc_sbuf_tensor("acc%d" % c, [128, T], BF16) for c in range(NDC)]
    wgb = nc.alloc_sbuf_tensor("wgb", [128, 64], BF16)
    eps_t = nc.alloc_sbuf_tensor("epsT", [128, 1], F32)
    bias16 = nc.alloc_sbuf_tensor("bias16", [128, 16], F32)
    rs_f = nc.alloc_sbuf_tensor("rsf", [1, 512], F32)
    rs_b = nc.alloc_sbuf_tensor("rsb", [1, 512], BF16)
    bc_sb = nc.alloc_sbuf_tensor("bcsb", [64, 512], F32)

    ident = consts[:, C_I:C_I + 128]
    rotT = consts[:, C_R:C_R + 128]
    ones128 = consts[:, C_ONE:C_ONE + 128]
    onescol64 = consts[0:64, C_ONE:C_ONE + 1]
    onesrow64 = consts[0:1, C_ONE:C_ONE + 64]
    sel8 = consts[0:8, C_SEL:C_SEL + 1024]

    phase_on = {p: True for p in PHASES}
    seen_last = False
    for p in PHASES:
        phase_on[p] = not seen_last
        if p == LAST_PHASE:
            seen_last = True

    s.dma(lambda e: e.dma_start(consts[:], d_cn[:]), writes=["consts"])
    s.dma(lambda e: e.dma_start(b1_all[:], d_b1[:]), writes=["b1all"])

    g_xf = [nc.sbuf_tensor("xf%d" % c, [128, T], F32) for c in range(NDC)]
    xf = [g.__enter__() for g in g_xf]
    g_more = [
        nc.sbuf_tensor("xb0", [128, T], BF16), nc.sbuf_tensor("xb1", [128, T], BF16),
        nc.sbuf_tensor("xq0", [128, T], BF16), nc.sbuf_tensor("xq1", [128, T], BF16),
        nc.sbuf_tensor("meanS", [128, T], F32), nc.sbuf_tensor("rstdS", [128, T], F32),
        nc.sbuf_tensor("vscr", [128, 512], F32), nc.sbuf_tensor("vscr2", [128, 512], F32),
        nc.sbuf_tensor("wgf", [128, 64], F32), nc.sbuf_tensor("bof", [8, 1024], F32),
    ]
    (xb0, xb1, xq0, xq1, meanS, rstdS, vscr, vscr2, wgf, bof) = [g.__enter__() for g in g_more]
    xb = [xb0, xb1]
    xq = [xq0, xq1]

    s.dma(lambda e: e.dma_start(wgf[:], d_wg[:]), writes=["wgf"])
    s.add("pool", lambda e: e.tensor_copy(wgb[:], wgf[:]), reads=["wgf"], writes=["wgb"])
    s.dma(lambda e: e.dma_start(bof[:], d_bo[:]), writes=["bof"])
    s.add("pool", lambda e: e.tensor_copy(boRep[:], bof[:]), reads=["bof"], writes=["boRep"])
    s.add("pool", lambda e: e.memset(eps_t[:], 1e-5), writes=["epsT"])

    g_lnps = [nc.psum_tensor("lnps%d" % i, [128, 512], F32) for i in range(8)]
    lnps = [g.__enter__() for g in g_lnps]

    for c in range(NDC):
        s.dma(lambda e, c=c: e.dma_start(xf[c][:], d_xT[c * 128:(c + 1) * 128, :]),
              writes=["xf%d" % c])
        s.add("dve", lambda e, c=c: e.tensor_copy(xb[c % 2][:], xf[c][:]),
              reads=["xf%d" % c], writes=["xb%d" % (c % 2)])
        s.add("act", lambda e, c=c: e.activation(xq[c % 2][:], xf[c][:], AF.Square),
              reads=["xf%d" % c], writes=["xq%d" % (c % 2)])
        for tt in range(NTT):
            st, sp = (c == 0), (c == NDC - 1)
            s.add("pe", lambda e, c=c, tt=tt, st=st, sp=sp: e.matmul(
                lnps[tt * 2][:], ones128, xb[c % 2][:, tt * 512:(tt + 1) * 512],
                start=st, stop=sp),
                reads=["xb%d" % (c % 2), "consts"], writes=["ps_ln%d" % (tt * 2)])
            s.add("pe", lambda e, c=c, tt=tt, st=st, sp=sp: e.matmul(
                lnps[tt * 2 + 1][:], ones128, xq[c % 2][:, tt * 512:(tt + 1) * 512],
                start=st, stop=sp),
                reads=["xq%d" % (c % 2), "consts"], writes=["ps_ln%d" % (tt * 2 + 1)])
    for tt in range(NTT):
        sl = slice(tt * 512, (tt + 1) * 512)
        s.add("dve", lambda e, tt=tt, sl=sl: e.tensor_scalar_mul(
            meanS[:, sl], lnps[tt * 2][:], 1.0 / D),
            reads=["ps_ln%d" % (tt * 2)], writes=["mean"])
        s.add("dve", lambda e, tt=tt, sl=sl: e.tensor_mul(
            vscr[:], meanS[:, sl], meanS[:, sl]), reads=["mean"], writes=["vscr"])
        s.add("dve", lambda e, tt=tt: e.scalar_tensor_tensor(
            vscr2[:], lnps[tt * 2 + 1][:], 1.0 / D, vscr[:], ALU.mult, ALU.subtract),
            reads=["ps_ln%d" % (tt * 2 + 1), "vscr"], writes=["vscr2"])
        s.add("act", lambda e, tt=tt: e.activation(
            vscr2[:], vscr2[:], AF.Ln, bias=eps_t[:, 0:1]),
            reads=["vscr2", "epsT"], writes=["vscr2"])
        s.add("act", lambda e, tt=tt, sl=sl: e.activation(
            rstdS[:, sl], vscr2[:], AF.Exp, scale=-0.5),
            reads=["vscr2"], writes=["rstd"])
    for c in range(NDC):
        s.add("dve", lambda e, c=c: e.tensor_sub(xf[c][:], xf[c][:], meanS[:]),
              reads=["xf%d" % c, "mean"], writes=["xf%d" % c])
    for c in range(NDC):
        s.add("dve", lambda e, c=c: e.tensor_mul(xn[c][:], xf[c][:], rstdS[:]),
              reads=["xf%d" % c, "rstd"], writes=["xn%d" % c])
    s.flush("ln")
    for g in reversed(g_lnps):
        g.__exit__(None, None, None)
    for g in reversed(g_more):
        g.__exit__(None, None, None)
    for g in reversed(g_xf):
        g.__exit__(None, None, None)

    if DEBUG_OUT and LAST_PHASE == "ln":
        _debug_dump(nc, s, dbg_out, {"xn": xn})
        nc.finalize()
        return nc

    g_gate = [nc.sbuf_tensor("g_%s%d" % (nm, i), [128, w],
                             BF16 if nm == "cb" else F32)
              for i in range(2)
              for (nm, w) in (("pr", 8), ("v", 8), ("sel", 8), ("cb", 8),
                              ("ex", 8), ("m1", 1), ("nm", 1), ("n1", 1),
                              ("m2", 1), ("tie", 1), ("den", 1), ("rd", 1))]
    gh_ = [g.__enter__() for g in g_gate]
    gt = [gh_[0:12], gh_[12:24]]
    GN = ("pr", "v", "sel", "cb", "ex", "m1", "nm", "n1", "m2", "tie", "den",
          "rd")
    gv = [dict(zip(GN, gt[i])) for i in range(2)]
    g_lg = [nc.psum_tensor("lgps%d" % i, [128, 8], F32) for i in range(2)]
    lgps = [g.__enter__() for g in g_lg]
    g_tr = [nc.psum_tensor("trps%d" % i, [8, 128], BF16) for i in range(2)]
    trps = [g.__enter__() for g in g_tr]

    def gate_chain(tc):
        p = tc % 2
        t = gv[p]
        P = "_%d" % p

        def A(eng, fn, reads, writes):
            return lambda: s.add(eng, fn, reads=reads, writes=writes)
        ops = []
        for cc in range(NDC):
            ops.append(A("pe", lambda e, tc=tc, cc=cc, p=p: e.matmul(
                lgps[p][:], xn[cc][:, tc * 128:(tc + 1) * 128],
                wgb[:, cc * 8:(cc + 1) * 8], start=(cc == 0),
                stop=(cc == NDC - 1)),
                ["xn%d" % cc, "wgb"], ["ps_lg" + P]))
        ops += [
            A("dve", lambda e, t=t, p=p: e.reduce_max(t["m1"][:], lgps[p][:],
                                                      axis=AX.X),
              ["ps_lg" + P], ["g_m1" + P]),
            A("dve", lambda e, t=t: e.tensor_scalar_mul(t["nm"][:], t["m1"][:],
                                                        -1.0),
              ["g_m1" + P], ["g_nm" + P]),
            A("act", lambda e, t=t, p=p: e.activation(
                t["pr"][:], lgps[p][:], AF.Exp, bias=t["nm"][:, 0:1]),
              ["ps_lg" + P, "g_nm" + P], ["g_pr" + P]),
            A("dve", lambda e, t=t: e.reduce_sum(t["m1"][:], t["pr"][:],
                                                 axis=AX.X),
              ["g_pr" + P], ["g_m1" + P]),
            A("dve", lambda e, t=t: e.reciprocal(t["rd"][:], t["m1"][:]),
              ["g_m1" + P], ["g_rd" + P]),
            A("dve", lambda e, t=t: e.tensor_scalar_mul(t["v"][:], t["pr"][:],
                                                        t["rd"][:, 0:1]),
              ["g_pr" + P, "g_rd" + P], ["g_v" + P]),
            A("dve", lambda e, t=t: e.reduce_max(t["m1"][:], t["v"][:],
                                                 axis=AX.X),
              ["g_v" + P], ["g_m1" + P]),
            A("dve", lambda e, t=t: e.tensor_single_scalar(
                t["sel"][:], t["v"][:], t["m1"][:, 0:1], ALU.is_ge),
              ["g_v" + P, "g_m1" + P], ["g_sel" + P]),
            A("dve", lambda e, t=t: e.reduce_sum(t["n1"][:], t["sel"][:],
                                                 axis=AX.X),
              ["g_sel" + P], ["g_n1" + P]),
            A("dve", lambda e, t=t: e.scalar_tensor_tensor(
                t["ex"][:], t["sel"][:], -1e9, t["v"][:], ALU.mult, ALU.add),
              ["g_sel" + P, "g_v" + P], ["g_ex" + P]),
            A("dve", lambda e, t=t: e.reduce_max(t["m2"][:], t["ex"][:],
                                                 axis=AX.X),
              ["g_ex" + P], ["g_m2" + P]),
            A("dve", lambda e, t=t: e.tensor_single_scalar(
                t["tie"][:], t["n1"][:], 1.5, ALU.is_ge),
              ["g_n1" + P], ["g_tie" + P]),
            A("dve", lambda e, t=t: e.tensor_sub(t["den"][:], t["m1"][:],
                                                 t["m2"][:]),
              ["g_m1" + P, "g_m2" + P], ["g_den" + P]),
            A("dve", lambda e, t=t: e.scalar_tensor_tensor(
                t["m2"][:], t["den"][:], t["tie"][:, 0:1], t["m2"][:],
                ALU.mult, ALU.add),
              ["g_den" + P, "g_tie" + P, "g_m2" + P], ["g_m2" + P]),
            A("dve", lambda e, t=t: e.tensor_scalar(
                t["den"][:], t["m1"][:], t["m2"][:, 0:1], 1e-9, ALU.add,
                ALU.add),
              ["g_m1" + P, "g_m2" + P], ["g_den" + P]),
            A("dve", lambda e, t=t: e.reciprocal(t["rd"][:], t["den"][:]),
              ["g_den" + P], ["g_rd" + P]),
            A("dve", lambda e, t=t: e.tensor_single_scalar(
                t["sel"][:], t["v"][:], t["m2"][:, 0:1], ALU.is_ge),
              ["g_v" + P, "g_m2" + P], ["g_sel" + P]),
            A("dve", lambda e, t=t: e.tensor_mul(t["v"][:], t["v"][:],
                                                 t["sel"][:]),
              ["g_v" + P, "g_sel" + P], ["g_v" + P]),
            A("dve", lambda e, t=t: e.tensor_scalar_mul(
                t["cb"][:], t["v"][:], t["rd"][:, 0:1]),
              ["g_v" + P, "g_rd" + P], ["g_cb" + P]),
            A("pe", lambda e, t=t, p=p: e.transpose(
                trps[p][:], t["cb"][:], ident),
              ["g_cb" + P, "consts"], ["ps_tr" + P]),
            A("dve", lambda e, tc=tc, p=p: e.tensor_copy(
                combT[:, tc * 128:(tc + 1) * 128], trps[p][:]),
              ["ps_tr" + P], ["combT"]),
        ]
        return ops

    for tcp in range(0, NTC, 2):
        a_ops = gate_chain(tcp)
        b_ops = gate_chain(tcp + 1)
        n = max(len(a_ops), len(b_ops))
        for i in range(n):
            if i < len(a_ops):
                a_ops[i]()
            if i < len(b_ops):
                b_ops[i]()
    s.flush("gate")
    for g in reversed(g_tr):
        g.__exit__(None, None, None)
    for g in reversed(g_lg):
        g.__exit__(None, None, None)
    for g in reversed(g_gate):
        g.__exit__(None, None, None)

    NPASS = 16
    WRN = 3
    g_m = (
        [nc.sbuf_tensor("qr%d" % i, [128, T], BF16) for i in range(2)] +
        [nc.sbuf_tensor("kr%d" % i, [128, T], BF16) for i in range(2)] +
        [nc.sbuf_tensor("tab%d" % i, [128, T], BF16) for i in range(4)] +
        [nc.sbuf_tensor("vaug%d" % i, [128, 2 * 16 * 65], BF16) for i in range(2)] +
        [nc.sbuf_tensor("Psb%d" % i, [128, 512], BF16) for i in range(4)] +
        [nc.sbuf_tensor("opair%d" % i, [128, T], BF16) for i in range(2)] +
        [nc.sbuf_tensor("wsf%d" % i, [128, 1024], F32) for i in range(3)] +
        [nc.sbuf_tensor("qkb%d" % i, [128, 1024], BF16) for i in range(2)] +
        [nc.sbuf_tensor("wvb", [128, 1024], BF16)] +
        [nc.sbuf_tensor("woutb%d" % i, [128, 1024], BF16) for i in range(2)] +
        [nc.sbuf_tensor("woq%d" % i, [128, 1024], BF16) for i in range(2)] +
        [nc.sbuf_tensor("w1b%d" % i, [128, 1024], BF16) for i in range(4)] +
        [nc.sbuf_tensor("cbs%d" % i, [128, T], BF16) for i in range(2)] +
        [nc.sbuf_tensor("asf%d" % i, [128, 1024], BF16) for i in range(2)] +
        [nc.sbuf_tensor("sgf%d" % i, [128, 1024], BF16) for i in range(2)] +
        [nc.sbuf_tensor("tsf%d" % i, [128, 1024], BF16) for i in range(2)] +
        [nc.sbuf_tensor("b1h", [128, 256], F32),
         nc.sbuf_tensor("rss", [1, 512], F32),
         nc.sbuf_tensor("gsum", [128, T], F32),
         nc.sbuf_tensor("gsb0", [128, T], BF16),
         nc.sbuf_tensor("gsb1", [128, T], BF16),
         nc.sbuf_tensor("ksqB0", [64, 512], BF16),
         nc.sbuf_tensor("ksqB1", [64, 512], BF16),
         nc.sbuf_tensor("nsc", [128, 8], F32),
         nc.sbuf_tensor("nln", [128, 2], F32),
         nc.sbuf_tensor("ropet", [128, 512], BF16)]
    )
    hm = [g.__enter__() for g in g_m]
    qr, kr, tabs, vaug = hm[0:2], hm[2:4], hm[4:8], hm[8:10]
    Psb, opair, wsf = hm[10:14], hm[14:16], hm[16:19]
    qkb, wvb = hm[19:21], hm[21]
    woutb, woqx, w1b = hm[22:24], hm[24:26], hm[26:30]
    cbs, asf, sgf, tsf = hm[30:32], hm[32:34], hm[34:36], hm[36:38]
    (b1h, rs_s, gsum, gsb0, gsb1, ksqB0, ksqB1, nsc, nln, t1) = hm[38:48]
    ksqX = [ksqB0, ksqB1]
    gsbx = [gsb0, gsb1]

    g_ps = [nc.psum_tensor("sps%d" % i, [128, 512], F32) for i in range(3)] + \
           [nc.psum_tensor("ops%d" % i, [65, 512], F32) for i in range(2)] + \
           [nc.psum_tensor("pjps", [128, 512], F32),
            nc.psum_tensor("hpa", [128, 512], F32),
            nc.psum_tensor("hpb", [128, 512], F32)]
    (sps0, sps1, sps2, ops0, ops1, pjps, hpa, hpb) = [g.__enter__() for g in g_ps]
    sps = [sps0, sps1, sps2]
    opsb = [ops0, ops1]
    hpx = [hpa, hpb]
    vwps = pjps
    ones64x128 = consts[0:64, C_ONE:C_ONE + 128]

    for i in range(4):
        s.dma(lambda e, i=i: e.dma_start(tabs[i][:], d_tab[i]),
              writes=["tab%d" % i])
    for dc in range(NDC):
        s.add("pool", lambda e, dc=dc: e.memset(acc[dc][:], 0.0),
              writes=["acc%d" % dc])
    s.add("dve", lambda e: e.tensor_scalar_mul(b1h[:], b1_all[:], 0.5),
          reads=["b1all"], writes=["b1h"])

    wring = [0]

    def wload(dst_ap, dst_tok, src_ap, tag, eng="pool"):
        i = wring[0] % WRN
        wring[0] += 1
        s.dma(lambda e, i=i, src_ap=src_ap: e.dma_start(wsf[i][:], src_ap),
              writes=["wsf%d" % i], tag=tag)
        if eng == "act":
            s.add("act", lambda e, i=i, dst_ap=dst_ap: e.activation(
                dst_ap, wsf[i][:], AF.Identity), reads=["wsf%d" % i],
                writes=[dst_tok])
        else:
            s.add("pool", lambda e, i=i, dst_ap=dst_ap: e.tensor_copy(
                dst_ap, wsf[i][:]), reads=["wsf%d" % i], writes=[dst_tok])

    pend = []

    def flush_pend(n=None):
        todo = pend[:n] if n else pend[:]
        del pend[:len(todo)]
        for fn in todo:
            fn()

    def gen_ffn():
        gh = [0]
        wload(woqx[0][:], "woq0", d_wf[0], "woqL0")
        for ex in range(E):
            cbp = ex % 2
            for tt in range(NTT):
                sl = slice(tt * 512, (tt + 1) * 512)
                cbk = tt % 2
                s.add("pe", lambda e, ex=ex, sl=sl, cbk=cbk: e.matmul(
                    hpx[cbk][:], sel8[:, ex * 128:(ex + 1) * 128],
                    combT[0:8, sl], start=True, stop=True),
                    reads=["combT", "consts"], writes=["ps_hp%d" % cbk])
                yield
                s.add("act", lambda e, cbp=cbp, sl=sl, cbk=cbk: e.activation(
                    cbs[cbp][:, sl], hpx[cbk][:], AF.Identity, scale=0.5),
                    reads=["ps_hp%d" % cbk], writes=["cbs%d" % cbp])
            s.dma(lambda e, ex=ex, cbp=cbp: e.dma_start(d_cbs[ex], cbs[cbp][:]),
                  reads=["cbs%d" % cbp], writes=["cbsout%d" % ex], tag="cbo%d" % ex)
        for p in range(NPASS):
            for ex in range(E):
                cbp = ex % 2
                s.dma(lambda e, ex=ex, cbp=cbp: e.dma_start(
                    cbs[cbp][:], d_cbs[ex]),
                    reads=["cbsout%d" % ex], writes=["cbs%d" % cbp],
                    tag="cbi_%d_%d" % (p, ex))
                ia = (2 * (p * E + ex)) % 4
                ib = ia + 1
                wload(w1b[ia][:], "w1b%d" % ia, d_w1[ex, p], "w1a_%d_%d" % (ex, p))
                wload(w1b[ib][:], "w1b%d" % ib, d_w1[ex, 16 + p],
                      "w1b_%d_%d" % (ex, p))
                b1a = b1_all[:, ex * 32 + p:ex * 32 + p + 1]
                b1b = b1h[:, ex * 32 + 16 + p:ex * 32 + 16 + p + 1]
                for hf in range(2):
                    hpar = gh[0] % 2
                    gh[0] += 1
                    for tq in range(2):
                        tt = 2 * hf + tq
                        sl = slice(tt * 512, (tt + 1) * 512)
                        qsl = slice(tq * 512, (tq + 1) * 512)
                        for cc in range(NDC):
                            s.add("pe", lambda e, ia=ia, cc=cc, sl=sl: e.matmul(
                                hpa[:], w1b[ia][:, cc * 128:(cc + 1) * 128],
                                xn[cc][:, sl], start=(cc == 0), stop=(cc == NDC - 1)),
                                reads=["w1b%d" % ia, "xn%d" % cc], writes=["ps_hp0"])
                            yield
                        s.add("dve", lambda e, sl=sl, qsl=qsl, hpar=hpar, b=b1a,
                              cbp=cbp: e.scalar_tensor_tensor(
                                  asf[hpar][:, qsl], hpa[:], b, cbs[cbp][:, sl],
                                  ALU.add, ALU.mult),
                              reads=["ps_hp0", "b1all", "cbs%d" % cbp],
                              writes=["as%d_%d" % (hpar, tq)])
                        flush_pend(1)
                        for cc in range(NDC):
                            s.add("pe", lambda e, ib=ib, cc=cc, sl=sl: e.matmul(
                                hpb[:], w1b[ib][:, cc * 128:(cc + 1) * 128],
                                xn[cc][:, sl], start=(cc == 0), stop=(cc == NDC - 1)),
                                reads=["w1b%d" % ib, "xn%d" % cc], writes=["ps_hp1"])
                            yield
                        s.add("act", lambda e, qsl=qsl, hpar=hpar, b=b1b:
                              e.activation(sgf[hpar][:, qsl], hpb[:], AF.Tanh,
                                           bias=b, scale=0.5),
                              reads=["ps_hp1", "b1h"],
                              writes=["sg%d_%d" % (hpar, tq)])
                        flush_pend(1)
                    hsl = slice(hf * 1024, (hf + 1) * 1024)
                    rd = ["as%d_%d" % (hpar, q) for q in range(2)] + \
                         ["sg%d_%d" % (hpar, q) for q in range(2)]
                    if ex == 0:
                        pend.append(lambda hsl=hsl, hpar=hpar, rd=rd, hf=hf: s.add(
                            "dve", lambda e: e.scalar_tensor_tensor(
                                gsum[:, hsl], sgf[hpar][:], 1.0, asf[hpar][:],
                                ALU.add, ALU.mult),
                            reads=rd, writes=["gs%d" % hf]))
                    elif ex < E - 1:
                        pend.append(lambda hpar=hpar, rd=rd: s.add(
                            "dve", lambda e: e.scalar_tensor_tensor(
                                tsf[hpar][:], sgf[hpar][:], 1.0, asf[hpar][:],
                                ALU.add, ALU.mult),
                            reads=rd, writes=["ts%d" % hpar]))
                        pend.append(lambda hsl=hsl, hpar=hpar, hf=hf: s.add(
                            "dve", lambda e: e.tensor_add(
                                gsum[:, hsl], tsf[hpar][:], gsum[:, hsl]),
                            reads=["ts%d" % hpar, "gs%d" % hf],
                            writes=["gs%d" % hf]))
                    else:
                        pend.append(lambda hpar=hpar, rd=rd: s.add(
                            "dve", lambda e: e.scalar_tensor_tensor(
                                tsf[hpar][:], sgf[hpar][:], 1.0, asf[hpar][:],
                                ALU.add, ALU.mult),
                            reads=rd, writes=["ts%d" % hpar]))
                        gx = p % 2
                        pend.append(lambda hsl=hsl, hpar=hpar, hf=hf, gx=gx: s.add(
                            "dve", lambda e: e.tensor_add(
                                gsbx[gx][:, hsl], tsf[hpar][:], gsum[:, hsl]),
                            reads=["ts%d" % hpar, "gs%d" % hf],
                            writes=["gq%d_%d" % (gx, hf)]))
            flush_pend()
            if p % 2 == 1:
                last = (p == NPASS - 1)
                for tt in range(NTT):
                    for dc in range(NDC):
                        sl = slice(tt * 512, (tt + 1) * 512)
                        ep = dc % 2
                        for gx in range(2):
                            s.add("pe", lambda e, dc=dc, sl=sl, ep=ep, gx=gx,
                                  last=last: e.matmul(
                                      hpx[ep][:],
                                      woqx[gx][:, dc * 128:(dc + 1) * 128],
                                      gsbx[gx][:, sl], start=(gx == 0),
                                      stop=(gx == 1 and not last)),
                                  reads=["woq%d" % gx, "gq%d_%d" % (gx, tt // 2)],
                                  writes=["ps_hp%d" % ep])
                            yield
                        if last:
                            s.add("pe", lambda e, dc=dc, sl=sl, ep=ep: e.matmul(
                                hpx[ep][:], boRep[0:8, dc * 128:(dc + 1) * 128],
                                combT[0:8, sl], start=False, stop=True),
                                reads=["boRep", "combT"], writes=["ps_hp%d" % ep])
                            yield
                        s.add("dve", lambda e, dc=dc, sl=sl, ep=ep: e.tensor_add(
                            acc[dc][:, sl], hpx[ep][:], acc[dc][:, sl]),
                            reads=["ps_hp%d" % ep, "acc%d" % dc],
                            writes=["acc%d" % dc])
            if p < NPASS - 1:
                wload(woqx[(p + 1) % 2][:], "woq%d" % ((p + 1) % 2),
                      d_wf[p + 1], "woq%d" % (p + 1))

    genf = gen_ffn()

    _DONE = object()

    def fill(n):
        for _ in range(n):
            if next(genf, _DONE) is _DONE:
                break

    def prologue_loads(c):
        wload(qkb[0][:], "qkb0", d_qk[c], "wq%d" % c)
        wload(qkb[1][:], "qkb1", d_qk[8 + c], "wk%d" % c)
        wload(wvb[:], "wvb", d_wv[c], "wv%d" % c)

    def prologue(c):
        r = c % 2
        for (wi, dst, dtok) in ((0, qr[r], "qr%d" % r), (1, kr[r], "kr%d" % r)):
            for tt in range(NTT):
                sl = slice(tt * 512, (tt + 1) * 512)
                for cc in range(NDC):
                    s.add("pe", lambda e, wi=wi, cc=cc, sl=sl: e.matmul(
                        pjps[:], qkb[wi][:, cc * 128:(cc + 1) * 128],
                        xn[cc][:, sl], start=(cc == 0), stop=(cc == NDC - 1)),
                        reads=["qkb%d" % wi, "xn%d" % cc], writes=["ps_pj"])
                fill(2)
                s.add("act", lambda e, dst=dst, sl=sl: e.activation(
                    dst[:, sl], pjps[:], AF.Identity),
                    reads=["ps_pj"], writes=[dtok])
                fill(1)
        for (dst, dtok, tc_, ts_) in ((qr[r], "qr%d" % r, 0, 1),
                                      (kr[r], "kr%d" % r, 2, 3)):
            for tt in range(NTT):
                sl = slice(tt * 512, (tt + 1) * 512)
                fill(1)
                s.add("pe", lambda e, dst=dst, sl=sl: e.matmul(
                    pjps[:], rotT, dst[:, sl], start=True, stop=True),
                    reads=[dtok, "consts"], writes=["ps_pj"])
                s.add("dve", lambda e, dst=dst, sl=sl, tc_=tc_:
                      e.tensor_mul(t1[:], dst[:, sl], tabs[tc_][:, sl]),
                      reads=[dtok, "tab%d" % tc_], writes=["ropet"])
                s.add("dve", lambda e, sl=sl, ts_=ts_:
                      e.tensor_mul(pjps[:], pjps[:], tabs[ts_][:, sl]),
                      reads=["ps_pj", "tab%d" % ts_], writes=["ps_pj"])
                s.add("dve", lambda e, dst=dst, sl=sl: e.tensor_add(
                    dst[:, sl], t1[:], pjps[:]),
                    reads=["ropet", "ps_pj"], writes=[dtok])
                fill(1)
        for hh in (0, 1):
            h = 2 * c + hh
            hl = hh * 64
            for (stile, stok, col0) in ((qr[r], "qr%d" % r, 0),
                                        (kr[r], "kr%d" % r, 4)):
                for lt in range(NTT):
                    sl = slice(lt * 512, (lt + 1) * 512)
                    kb = lt % 2
                    s.add("act", lambda e, stile=stile, hl=hl, sl=sl, kb=kb:
                          e.activation(
                              ksqX[kb][:], stile[hl:hl + 64, sl], AF.Square),
                          reads=[stok], writes=["ksqB%d" % kb])
                    fill(1)
                    s.add("pe", lambda e, kb=kb: e.matmul(
                        pjps[:], ones64x128, ksqX[kb][:], start=True, stop=True),
                        reads=["ksqB%d" % kb, "consts"], writes=["ps_pj"])
                    s.add("dve", lambda e, col=col0 + lt: e.reduce_max(
                        nsc[:, col:col + 1], pjps[:], axis=AX.X),
                        reads=["ps_pj"], writes=["nsc%d" % (col0 + lt)])
            s.add("dve", lambda e: e.reduce_max(nln[:, 0:1], nsc[:, 0:4], axis=AX.X),
                  reads=["nsc0", "nsc1", "nsc2", "nsc3"], writes=["nl0"])
            s.add("dve", lambda e: e.reduce_max(nln[:, 1:2], nsc[:, 4:8], axis=AX.X),
                  reads=["nsc4", "nsc5", "nsc6", "nsc7"], writes=["nl1"])
            s.add("dve", lambda e: e.tensor_scalar_mul(
                nln[:, 0:1], nln[:, 0:1], -4.0),
                reads=["nl0"], writes=["nl0"])
            s.add("dve", lambda e, h=h: e.scalar_tensor_tensor(
                bias16[:, h:h + 1], nln[:, 1:2], -0.0625, nln[:, 0:1],
                ALU.mult, ALU.add),
                reads=["nl0", "nl1"], writes=["bias%d" % h])
        ones_ap = vaug[r][:, :].rearrange("p (g t s) -> p g t s", g=2, t=16,
                                          s=65)[:, :, :, 64:65]
        s.add("pool", lambda e, ones_ap=ones_ap: e.memset(ones_ap, 1.0),
              writes=["vaug%d" % r])
        for tc in range(NTC):
            fill(1)
            for cc in range(NDC):
                s.add("pe", lambda e, tc=tc, cc=cc: e.matmul(
                    sps[2][:, 0:128], xn[cc][:, tc * 128:(tc + 1) * 128],
                    wvb[:, cc * 128:(cc + 1) * 128],
                    start=(cc == 0), stop=(cc == NDC - 1)),
                    reads=["wvb", "xn%d" % cc], writes=["ps_s2"])
            fill(2)
            dstv = vaug[r][:, :].rearrange("p (g rr) -> p g rr", g=2, rr=1040)[
                :, :, tc * 65:tc * 65 + 64]
            srcv = sps[2][:, 0:128].rearrange("p (g x) -> p g x", g=2, x=64)
            s.add("act", lambda e, dstv=dstv, srcv=srcv: e.activation(
                dstv, srcv, AF.Identity),
                reads=["ps_s2"], writes=["vaug%d" % r])
            fill(1)

    apend = []

    def flush_anorm():
        while apend:
            apend.pop(0)[1]()

    def FILLQ(mc):
        fill(6)

    def head_lt(h, lt):
        c, hl = h // 2, (h % 2) * 64
        r, vp, hg = c % 2, c % 2, h % 2
        qtok, ktok = "qr%d" % r, "kr%d" % r
        sl = slice(lt * 512, (lt + 1) * 512)
        ob = lt % 2

        def _smm(mc):
            sb = mc % 4
            spb = sps[sb] if sb < 3 else pjps
            stok = ("ps_s%d" % sb) if sb < 3 else "ps_pj"
            s.add("pe", lambda e, mc=mc, spb=spb, stok=stok: e.matmul(
                spb[:], kr[r][hl:hl + 64, mc * 128:(mc + 1) * 128],
                qr[r][hl:hl + 64, sl], start=True, stop=True),
                reads=[ktok, qtok], writes=[stok])
            s.add("act", lambda e, sb=sb, spb=spb, stok=stok: e.activation(
                Psb[sb][:], spb[:], AF.Exp, bias=bias16[:, h:h + 1]),
                reads=[stok, "bias%d" % h], writes=["Psb%d" % sb])

        _smm(0)
        _smm(1)
        _smm(2)
        _smm(3)
        for mc in range(NTC):
            pb = mc % 4
            FILLQ(mc)
            s.add("pe", lambda e, mc=mc, pb=pb: e.matmul(
                opsb[ob][:],
                vaug[vp][:, hg * 1040 + mc * 65:hg * 1040 + mc * 65 + 65],
                Psb[pb][:], start=(mc == 0), stop=(mc == NTC - 1)),
                reads=["vaug%d" % vp, "Psb%d" % pb], writes=["ps_o%d" % ob])
            if mc + 4 < NTC:
                _smm(mc + 4)
            if mc == 1:
                while apend and apend[0][0] == 0:
                    apend.pop(0)[1]()
            if mc == 14:
                flush_anorm()
        apend.append((0, lambda ob=ob: s.add(
            "dve", lambda e: e.reciprocal(rs_f[:], opsb[ob][64:65, :]),
            reads=["ps_o%d" % ob], writes=["rsf"])))
        apend.append((1, lambda: s.add(
            "act", lambda e: e.activation(rs_b[:], rs_f[:], AF.Identity),
            reads=["rsf"], writes=["rsb"])))
        apend.append((1, lambda: s.add(
            "pe", lambda e: e.matmul(pjps[0:64, :], onesrow64, rs_b[:],
                                     start=True, stop=True),
            reads=["rsb", "consts"], writes=["ps_pj"])))
        apend.append((1, lambda: s.add(
            "act", lambda e: e.activation(bc_sb[:], pjps[0:64, :], AF.Identity),
            reads=["ps_pj"], writes=["bcsb"])))
        apend.append((1, lambda c=c, hl=hl, sl=sl, ob=ob: s.add(
            "dve", lambda e: e.tensor_mul(
                opair[c % 2][hl:hl + 64, sl], opsb[ob][0:64, :], bc_sb[:]),
            reads=["ps_o%d" % ob, "bcsb"], writes=["opair%d" % (c % 2)])))

    def outproj(c):
        flush_anorm()
        for tt in range(NTT):
            for dc in range(NDC):
                sl = slice(tt * 512, (tt + 1) * 512)
                op_ = dc % 2
                s.add("pe", lambda e, c=c, dc=dc, sl=sl, op_=op_: e.matmul(
                    sps[op_][:], woutb[c % 2][:, dc * 128:(dc + 1) * 128],
                    opair[c % 2][:, sl], start=True, stop=True),
                    reads=["woutb%d" % (c % 2), "opair%d" % (c % 2)],
                    writes=["ps_s%d" % op_])
                fill(2)
                s.add("dve", lambda e, dc=dc, sl=sl, op_=op_: e.tensor_add(
                    acc[dc][:, sl], sps[op_][:], acc[dc][:, sl]),
                    reads=["ps_s%d" % op_, "acc%d" % dc], writes=["acc%d" % dc])
                fill(1)

    prologue_loads(0)
    prologue(0)
    for c in range(NDC):
        wload(woutb[c % 2][:], "woutb%d" % (c % 2), d_wo[c], "wout%d" % c)
        if c + 1 < NDC:
            prologue_loads(c + 1)
        for lt in range(NTT):
            head_lt(2 * c, lt)
        if c + 1 < NDC:
            prologue(c + 1)
        for lt in range(NTT):
            head_lt(2 * c + 1, lt)
        outproj(c)
    for _ in genf:
        pass
    flush_pend()
    s.flush("merged")
    for g in reversed(g_ps):
        g.__exit__(None, None, None)
    for g in reversed(g_m):
        g.__exit__(None, None, None)

    if DEBUG_OUT and LAST_PHASE in ("qkv", "attn", "ffn"):
        _debug_dump(nc, s, dbg_out, {"acc": acc})
        nc.finalize()
        return nc


    g_fin = [nc.sbuf_tensor("fxf%d" % i, [128, T], F32) for i in range(4)] + \
            [nc.sbuf_tensor("fout%d" % i, [128, T], F32) for i in range(4)]
    fxf = [g.__enter__() for g in g_fin[0:4]]
    fout = [g.__enter__() for g in g_fin[4:8]]
    for dc in range(NDC):
        p = dc % 4
        s.dma(lambda e, dc=dc, p=p: e.dma_start(
            fxf[p][:], d_xT[dc * 128:(dc + 1) * 128, :]),
            reads=["fout%d" % p], writes=["fxf%d" % p])
        s.add("dve", lambda e, dc=dc, p=p: e.tensor_add(
            fout[p][:], fxf[p][:], acc[dc][:]),
            reads=["fxf%d" % p, "acc%d" % dc], writes=["fout%d" % p])
        s.dma(lambda e, dc=dc, p=p: e.dma_start(
            d_out[dc * 128:(dc + 1) * 128, :], fout[p][:]),
            reads=["fout%d" % p], writes=["out%d" % dc], eng="pool")
    s.add("pool", lambda e: e.nop(),
          reads=["out%d" % dc for dc in range(NDC)], tag="fence")
    s.flush("final")
    for g in reversed(g_fin):
        g.__exit__(None, None, None)

    nc.finalize()
    return nc


def _emit_vpair(s, nc, pair, d_wv, wvf, wvb, vaug, xn, vps):
    p = pair % 2
    s.dma(lambda e, pair=pair, p=p: e.dma_start(wvf[p][:], d_wv[pair]),
          reads=["wvf%d" % p], writes=["wvf%d" % p], tag="wv%d" % pair)
    s.add("pool", lambda e, p=p: e.tensor_copy(wvb[p][:], wvf[p][:]),
          reads=["wvf%d" % p], writes=["wvb%d" % p])
    ones_ap = vaug[p][:, :].rearrange("p (g t s) -> p g t s", g=2, t=16, s=65)[
        :, :, :, 64:65]
    s.add("pool", lambda e, ones_ap=ones_ap: e.memset(ones_ap, 1.0),
          writes=["vaug%d" % p])
    for tc in range(NTC):
        for c in range(NDC):
            s.add("pe", lambda e, p=p, tc=tc, c=c: e.matmul(
                vps[:], xn[c][:, tc * 128:(tc + 1) * 128],
                wvb[p][:, c * 128:(c + 1) * 128],
                start=(c == 0), stop=(c == NDC - 1)),
                reads=["wvb%d" % p, "xn%d" % c], writes=["ps_v"])
        dst = vaug[p][:, :].rearrange("p (g r) -> p g r", g=2, r=1040)[
            :, :, tc * 65:tc * 65 + 64]
        src = vps[:].rearrange("p (g x) -> p g x", g=2, x=64)
        s.add("act", lambda e, dst=dst, src=src: e.activation(
            dst, src, AF.Identity),
            reads=["ps_v"], writes=["vaug%d" % p])


def _debug_dump(nc, s, dbg_out, tensors):
    row = 0
    guards = [nc.sbuf_tensor("dbgcvt", [128, 2560], F32)]
    cvt = guards[0].__enter__()
    for name, tiles in tensors.items():
        for t in tiles:
            npart = t[:].shape[0]
            fw = t[:].shape[-1]
            s.add("dve", lambda e, t=t, npart=npart, fw=fw: e.tensor_copy(
                cvt[0:npart, 0:fw], t[:]), reads=["dbgcvt"], writes=["dbgcvt"])
            s.dma(lambda e, row=row, npart=npart, fw=fw: e.dma_start(
                dbg_out[row:row + npart, 0:fw], cvt[0:npart, 0:fw]),
                reads=["dbgcvt"], writes=["dbg"])
            row += npart
    s.add("pool", lambda e: e.nop(), reads=["dbg"], tag="fence")
    s.flush("debug")
    for g in reversed(guards):
        g.__exit__(None, None, None)



def _host_prep(L0_attn_x, L0_attn_w_qkv, L0_attn_w_out, L0_ffn_Wg, L0_ffn_W1,
               L0_ffn_b1, L0_ffn_Wo, L0_ffn_bo):
    x = np.asarray(L0_attn_x, dtype=np.float32)
    wqkv = np.asarray(L0_attn_w_qkv, dtype=np.float32)
    wout = np.asarray(L0_attn_w_out, dtype=np.float32)
    Wg = np.asarray(L0_ffn_Wg, dtype=np.float32)
    W1 = np.asarray(L0_ffn_W1, dtype=np.float32)
    b1 = np.asarray(L0_ffn_b1, dtype=np.float32)
    Wo = np.asarray(L0_ffn_Wo, dtype=np.float32)
    bo = np.asarray(L0_ffn_bo, dtype=np.float32)

    xT = np.ascontiguousarray(x.transpose(0, 2, 1))
    WT = np.ascontiguousarray(wqkv.T)
    wqkS = np.ascontiguousarray(
        WT[:, :2048].reshape(8, 128, 16, 128).transpose(2, 1, 0, 3)
    ).reshape(16, 128, 1024)
    wvS = np.ascontiguousarray(
        WT[:, 2048:].reshape(8, 128, 8, 128).transpose(2, 1, 0, 3)
    ).reshape(8, 128, 1024)
    woutS = np.ascontiguousarray(
        wout.T.reshape(8, 128, 8, 128)).reshape(8, 128, 1024)
    wgS = np.ascontiguousarray(Wg.T.reshape(8, 128, 8).transpose(1, 0, 2)
                               ).reshape(128, 64)
    w1S = np.ascontiguousarray(
        W1.transpose(0, 2, 1).reshape(E, 8, 128, 32, 128).transpose(0, 3, 2, 1, 4)
    ).reshape(E, 32, 128, 1024)
    wfS = np.ascontiguousarray(Wo.T.reshape(16, 128, 8, 128)).reshape(16, 128, 1024)
    b1S = np.ascontiguousarray(b1.reshape(8, 32, 128).transpose(2, 0, 1)
                               ).reshape(128, 256)
    boS = np.ascontiguousarray(np.broadcast_to(bo, (8, 1024)))

    consts = np.zeros((128, 1410), dtype=np.float32)
    consts[:, 0:128] = np.eye(128)
    R = np.zeros((64, 64), dtype=np.float32)
    for j in range(32):
        R[j, j + 32] = -1.0
        R[j + 32, j] = 1.0
    R2 = np.zeros((128, 128), dtype=np.float32)
    R2[0:64, 0:64] = R
    R2[64:128, 64:128] = R
    consts[:, 128:256] = R2.T
    consts[:, 256:384] = 1.0
    consts[0:64, 384] = 1.0
    consts[64:128, 385] = 1.0
    for ex in range(8):
        consts[ex, 386 + ex * 128:386 + (ex + 1) * 128] = 1.0
    constS = consts.astype(BF)

    half = 32
    inv = 1.0 / (10000.0 ** (np.arange(half, dtype=np.float64) / half))
    ang = np.arange(T, dtype=np.float64)[:, None] * inv[None, :]
    cos = np.cos(ang)
    sin = np.sin(ang)
    cos64 = np.concatenate([cos, cos], axis=1).T
    sin64 = np.concatenate([sin, sin], axis=1).T
    cos128 = np.concatenate([cos64, cos64], axis=0)
    sin128 = np.concatenate([sin64, sin64], axis=0)
    tabS = np.stack([
        cos128 * 0.125, sin128 * 0.125, cos128, sin128
    ]).astype(BF)

    shared = {"wqkS": wqkS, "wvS": wvS, "woutS": woutS, "wgS": wgS,
              "w1S": w1S, "wfS": wfS, "b1S": b1S, "boS": boS,
              "constS": constS, "tabS": tabS}
    in_maps = [dict(shared, xT=np.ascontiguousarray(xT[c])) for c in range(8)]
    return in_maps


def kernel(**inputs):
    in_maps = _host_prep(**inputs)
    nc = build_program()
    res = run_bass_kernel_spmd(nc, in_maps, core_ids=list(range(8)))
    outs = []
    for c in range(8):
        key = "dbgT" if DEBUG_OUT else "outT"
        o = res.results[c][key]
        outs.append(o if DEBUG_OUT else np.ascontiguousarray(o.T))
    return np.stack(outs).astype(np.float32)


# version 52 (final)
# speedup vs baseline: 1.0276x; 1.0003x over previous
import numpy as np
import ml_dtypes
import concourse.bass as bass
import concourse.mybir as mybir
from concourse.bass_utils import run_bass_kernel_spmd

F32 = mybir.dt.float32
BF16 = mybir.dt.bfloat16
AF = mybir.ActivationFunctionType
ALU = mybir.AluOpType
AX = mybir.AxisListType
BF = ml_dtypes.bfloat16

T = 2048
D = 1024
NH = 16
DH = 64
E = 8
HF = 4096
NTT = 4
NDC = 8
NTC = 16
NJ = 32

PHASES = ("ln", "gate", "qkv", "attn", "ffn", "final")
LAST_PHASE = "final"
DEBUG_OUT = None


ENGINES = ("sync", "pe", "act", "dve", "pool")


SAFE_DIST = 2


class _Op:
    __slots__ = ("eng", "fn", "deps", "marks", "idx", "inc_val", "tag", "drain",
                 "sem")

    def __init__(self, eng, fn, tag):
        self.eng = eng
        self.fn = fn
        self.deps = set()
        self.marks = []
        self.idx = -1
        self.inc_val = 0
        self.tag = tag
        self.drain = False
        self.sem = None


class Sched:
    def __init__(self, nc):
        self.nc = nc
        self.sem = {e: nc.alloc_semaphore("sched_" + e) for e in ENGINES}
        self.count = {e: 0 for e in ENGINES}
        self.dma_sems = {}
        self.ops = []
        self.writer = {}
        self.readers = {}
        self.nblk = 0

    def _dep(self, op, item):
        if isinstance(item, _Op):
            if item.eng == op.eng:
                op.deps.add(item)
            else:
                op.deps.add(item)
        else:
            op.marks.append(item)

    def add(self, eng, fn, reads=(), writes=(), tag=""):
        op = _Op(eng, fn, tag)
        for t in reads:
            w = self.writer.get(t)
            if w is not None:
                self._dep(op, w)
            self.readers.setdefault(t, []).append(op)
        for t in writes:
            w = self.writer.get(t)
            if w is not None:
                self._dep(op, w)
            for r in self.readers.get(t, ()):
                if r is not op:
                    self._dep(op, r)
            self.writer[t] = op
            self.readers[t] = []
        self.ops.append(op)
        return op

    def dma(self, fn, reads=(), writes=(), tag="", eng="sync"):
        op = self.add(eng, fn, reads, writes, tag)
        key = writes[0] if writes else tag
        if key not in self.dma_sems:
            self.dma_sems[key] = [self.nc.alloc_semaphore("dma_" + key), 0]
        op.sem = self.dma_sems[key]
        return op

    def flush(self, name):
        ops = self.ops
        per = {e: [] for e in ENGINES}
        for op in ops:
            per[op.eng].append(op)
            op.idx = len(per[op.eng]) - 1
        need = set()
        for op in ops:
            for d in op.deps:
                if isinstance(d, _Op) and d.eng != op.eng:
                    need.add(d)
        for e in ENGINES:
            for op in per[e]:
                if op.sem is not None:
                    op.sem[1] += 16
                    op.inc_val = op.sem[1]
                elif op in need:
                    self.count[e] += 1
                    op.inc_val = self.count[e]
        for e in ("act", "dve", "pool"):
            drained = -1
            for op in per[e]:
                for d in op.deps:
                    if d.eng == e and d.idx > drained and op.idx - d.idx < SAFE_DIST:
                        op.drain = True
                        drained = op.idx - 1
                        break
        blk = self.nc.Block("blk%d_%s" % (self.nblk, name))
        self.nblk += 1
        sched = self

        def emit(eng_name):
            def body(eng):
                seen = {}
                for op in per[eng_name]:
                    waits = {}
                    for d in op.deps:
                        if d.eng != eng_name and d.inc_val:
                            s = d.sem[0] if d.sem is not None else sched.sem[d.eng]
                            waits[s] = max(waits.get(s, 0), d.inc_val)
                    for (s, v) in op.marks:
                        waits[s] = max(waits.get(s, 0), v)
                    for s, v in waits.items():
                        if seen.get(s, -1) >= v:
                            continue
                        seen[s] = v
                        eng.wait_ge(s, v)
                    if op.drain:
                        eng.drain()
                    instr = op.fn(eng)
                    if op.inc_val:
                        if op.sem is not None:
                            instr.then_inc(op.sem[0], 16)
                        else:
                            instr.then_inc(sched.sem[eng_name], 1)
            return body

        with blk:
            if per["sync"]:
                blk.sync(emit("sync"))
            if per["pe"]:
                blk.tensor(emit("pe"))
            if per["act"]:
                blk.scalar(emit("act"))
            if per["dve"]:
                blk.vector(emit("dve"))
            if per["pool"]:
                blk.gpsimd(emit("pool"))
        neww, newr = {}, {}
        for t, w in self.writer.items():
            if isinstance(w, _Op):
                if w.eng == "sync" and w.inc_val:
                    neww[t] = (w.sem[0], w.inc_val)
            else:
                neww[t] = w
        self.writer = neww
        self.readers = newr
        self.ops = []



def build_program():
    nc = bass.Bass("TRN2", target_bir_lowering=False)
    s = Sched(nc)

    d_xT = nc.dram_tensor("xT", [D, T], F32, kind="ExternalInput").ap()
    d_qk = nc.dram_tensor("wqkS", [16, 128, 1024], F32, kind="ExternalInput").ap()
    d_wv = nc.dram_tensor("wvS", [8, 128, 1024], F32, kind="ExternalInput").ap()
    d_wo = nc.dram_tensor("woutS", [8, 128, 1024], F32, kind="ExternalInput").ap()
    d_wg = nc.dram_tensor("wgS", [128, 64], F32, kind="ExternalInput").ap()
    d_w1 = nc.dram_tensor("w1S", [E, NJ, 128, 1024], F32, kind="ExternalInput").ap()
    d_wf = nc.dram_tensor("wfS", [16, 128, 1024], F32, kind="ExternalInput").ap()
    d_b1 = nc.dram_tensor("b1S", [128, 256], F32, kind="ExternalInput").ap()
    d_bo = nc.dram_tensor("boS", [8, 1024], F32, kind="ExternalInput").ap()
    d_cn = nc.dram_tensor("constS", [128, 1410], BF16, kind="ExternalInput").ap()
    d_tab = nc.dram_tensor("tabS", [4, 128, T], BF16, kind="ExternalInput").ap()
    d_cbs = nc.dram_tensor("cbsScratch", [E, 128, T], BF16, kind="Internal").ap()
    d_out = nc.dram_tensor("outT", [D, T], F32, kind="ExternalOutput").ap()
    dbg_out = None
    if DEBUG_OUT:
        dbg_out = nc.dram_tensor("dbgT", [4096, 2560], F32, kind="ExternalOutput").ap()

    C_I, C_R, C_ONE, C_HS, C_SEL = 0, 128, 256, 384, 386

    xn = [nc.alloc_sbuf_tensor("xn%d" % c, [128, T], BF16) for c in range(NDC)]
    combT = nc.alloc_sbuf_tensor("combT", [8, T], BF16)
    b1_all = nc.alloc_sbuf_tensor("b1all", [128, 256], F32)
    boRep = nc.alloc_sbuf_tensor("boRep", [8, 1024], BF16)
    consts = nc.alloc_sbuf_tensor("consts", [128, 1410], BF16)
    acc = [nc.alloc_sbuf_tensor("acc%d" % c, [128, T], BF16) for c in range(NDC)]
    wgb = nc.alloc_sbuf_tensor("wgb", [128, 64], BF16)
    eps_t = nc.alloc_sbuf_tensor("epsT", [128, 1], F32)
    bias16 = nc.alloc_sbuf_tensor("bias16", [128, 16], F32)
    rs_f = nc.alloc_sbuf_tensor("rsf", [1, 512], F32)
    rs_b = nc.alloc_sbuf_tensor("rsb", [1, 512], BF16)
    bc_sb = nc.alloc_sbuf_tensor("bcsb", [64, 512], F32)

    ident = consts[:, C_I:C_I + 128]
    rotT = consts[:, C_R:C_R + 128]
    ones128 = consts[:, C_ONE:C_ONE + 128]
    onescol64 = consts[0:64, C_ONE:C_ONE + 1]
    onesrow64 = consts[0:1, C_ONE:C_ONE + 64]
    sel8 = consts[0:8, C_SEL:C_SEL + 1024]

    phase_on = {p: True for p in PHASES}
    seen_last = False
    for p in PHASES:
        phase_on[p] = not seen_last
        if p == LAST_PHASE:
            seen_last = True

    s.dma(lambda e: e.dma_start(consts[:], d_cn[:]), writes=["consts"])
    s.dma(lambda e: e.dma_start(b1_all[:], d_b1[:]), writes=["b1all"])

    g_xf = [nc.sbuf_tensor("xf%d" % c, [128, T], F32) for c in range(NDC)]
    xf = [g.__enter__() for g in g_xf]
    g_more = [
        nc.sbuf_tensor("xb0", [128, T], BF16), nc.sbuf_tensor("xb1", [128, T], BF16),
        nc.sbuf_tensor("xq0", [128, T], BF16), nc.sbuf_tensor("xq1", [128, T], BF16),
        nc.sbuf_tensor("meanS", [128, T], F32), nc.sbuf_tensor("rstdS", [128, T], F32),
        nc.sbuf_tensor("vscr", [128, 512], F32), nc.sbuf_tensor("vscr2", [128, 512], F32),
        nc.sbuf_tensor("wgf", [128, 64], F32), nc.sbuf_tensor("bof", [8, 1024], F32),
    ]
    (xb0, xb1, xq0, xq1, meanS, rstdS, vscr, vscr2, wgf, bof) = [g.__enter__() for g in g_more]
    xb = [xb0, xb1]
    xq = [xq0, xq1]

    s.dma(lambda e: e.dma_start(wgf[:], d_wg[:]), writes=["wgf"])
    s.add("pool", lambda e: e.tensor_copy(wgb[:], wgf[:]), reads=["wgf"], writes=["wgb"])
    s.dma(lambda e: e.dma_start(bof[:], d_bo[:]), writes=["bof"])
    s.add("pool", lambda e: e.tensor_copy(boRep[:], bof[:]), reads=["bof"], writes=["boRep"])
    s.add("pool", lambda e: e.memset(eps_t[:], 1e-5), writes=["epsT"])

    g_lnps = [nc.psum_tensor("lnps%d" % i, [128, 512], F32) for i in range(8)]
    lnps = [g.__enter__() for g in g_lnps]

    for c in range(NDC):
        s.dma(lambda e, c=c: e.dma_start(xf[c][:], d_xT[c * 128:(c + 1) * 128, :]),
              writes=["xf%d" % c])
        s.add("dve", lambda e, c=c: e.tensor_copy(xb[c % 2][:], xf[c][:]),
              reads=["xf%d" % c], writes=["xb%d" % (c % 2)])
        s.add("act", lambda e, c=c: e.activation(xq[c % 2][:], xf[c][:], AF.Square),
              reads=["xf%d" % c], writes=["xq%d" % (c % 2)])
        for tt in range(NTT):
            st, sp = (c == 0), (c == NDC - 1)
            s.add("pe", lambda e, c=c, tt=tt, st=st, sp=sp: e.matmul(
                lnps[tt * 2][:], ones128, xb[c % 2][:, tt * 512:(tt + 1) * 512],
                start=st, stop=sp),
                reads=["xb%d" % (c % 2), "consts"], writes=["ps_ln%d" % (tt * 2)])
            s.add("pe", lambda e, c=c, tt=tt, st=st, sp=sp: e.matmul(
                lnps[tt * 2 + 1][:], ones128, xq[c % 2][:, tt * 512:(tt + 1) * 512],
                start=st, stop=sp),
                reads=["xq%d" % (c % 2), "consts"], writes=["ps_ln%d" % (tt * 2 + 1)])
    for tt in range(NTT):
        sl = slice(tt * 512, (tt + 1) * 512)
        s.add("dve", lambda e, tt=tt, sl=sl: e.tensor_scalar_mul(
            meanS[:, sl], lnps[tt * 2][:], 1.0 / D),
            reads=["ps_ln%d" % (tt * 2)], writes=["mean"])
        s.add("dve", lambda e, tt=tt, sl=sl: e.tensor_mul(
            vscr[:], meanS[:, sl], meanS[:, sl]), reads=["mean"], writes=["vscr"])
        s.add("dve", lambda e, tt=tt: e.scalar_tensor_tensor(
            vscr2[:], lnps[tt * 2 + 1][:], 1.0 / D, vscr[:], ALU.mult, ALU.subtract),
            reads=["ps_ln%d" % (tt * 2 + 1), "vscr"], writes=["vscr2"])
        s.add("act", lambda e, tt=tt: e.activation(
            vscr2[:], vscr2[:], AF.Ln, bias=eps_t[:, 0:1]),
            reads=["vscr2", "epsT"], writes=["vscr2"])
        s.add("act", lambda e, tt=tt, sl=sl: e.activation(
            rstdS[:, sl], vscr2[:], AF.Exp, scale=-0.5),
            reads=["vscr2"], writes=["rstd"])
    for c in range(NDC):
        s.add("dve", lambda e, c=c: e.tensor_sub(xf[c][:], xf[c][:], meanS[:]),
              reads=["xf%d" % c, "mean"], writes=["xf%d" % c])
    for c in range(NDC):
        s.add("dve", lambda e, c=c: e.tensor_mul(xn[c][:], xf[c][:], rstdS[:]),
              reads=["xf%d" % c, "rstd"], writes=["xn%d" % c])
    s.flush("ln")
    for g in reversed(g_lnps):
        g.__exit__(None, None, None)
    for g in reversed(g_more):
        g.__exit__(None, None, None)
    for g in reversed(g_xf):
        g.__exit__(None, None, None)

    if DEBUG_OUT and LAST_PHASE == "ln":
        _debug_dump(nc, s, dbg_out, {"xn": xn})
        nc.finalize()
        return nc

    g_gate = [nc.sbuf_tensor("g_%s%d" % (nm, i), [128, w],
                             BF16 if nm == "cb" else F32)
              for i in range(2)
              for (nm, w) in (("pr", 8), ("v", 8), ("sel", 8), ("cb", 8),
                              ("ex", 8), ("m1", 1), ("nm", 1), ("n1", 1),
                              ("m2", 1), ("tie", 1), ("den", 1), ("rd", 1))]
    gh_ = [g.__enter__() for g in g_gate]
    gt = [gh_[0:12], gh_[12:24]]
    GN = ("pr", "v", "sel", "cb", "ex", "m1", "nm", "n1", "m2", "tie", "den",
          "rd")
    gv = [dict(zip(GN, gt[i])) for i in range(2)]
    g_lg = [nc.psum_tensor("lgps%d" % i, [128, 8], F32) for i in range(2)]
    lgps = [g.__enter__() for g in g_lg]
    g_tr = [nc.psum_tensor("trps%d" % i, [8, 128], BF16) for i in range(2)]
    trps = [g.__enter__() for g in g_tr]

    def gate_chain(tc):
        p = tc % 2
        t = gv[p]
        P = "_%d" % p

        def A(eng, fn, reads, writes):
            return lambda: s.add(eng, fn, reads=reads, writes=writes)
        ops = []
        for cc in range(NDC):
            ops.append(A("pe", lambda e, tc=tc, cc=cc, p=p: e.matmul(
                lgps[p][:], xn[cc][:, tc * 128:(tc + 1) * 128],
                wgb[:, cc * 8:(cc + 1) * 8], start=(cc == 0),
                stop=(cc == NDC - 1)),
                ["xn%d" % cc, "wgb"], ["ps_lg" + P]))
        ops += [
            A("dve", lambda e, t=t, p=p: e.reduce_max(t["m1"][:], lgps[p][:],
                                                      axis=AX.X),
              ["ps_lg" + P], ["g_m1" + P]),
            A("dve", lambda e, t=t: e.tensor_scalar_mul(t["nm"][:], t["m1"][:],
                                                        -1.0),
              ["g_m1" + P], ["g_nm" + P]),
            A("act", lambda e, t=t, p=p: e.activation(
                t["pr"][:], lgps[p][:], AF.Exp, bias=t["nm"][:, 0:1]),
              ["ps_lg" + P, "g_nm" + P], ["g_pr" + P]),
            A("dve", lambda e, t=t: e.reduce_sum(t["m1"][:], t["pr"][:],
                                                 axis=AX.X),
              ["g_pr" + P], ["g_m1" + P]),
            A("dve", lambda e, t=t: e.reciprocal(t["rd"][:], t["m1"][:]),
              ["g_m1" + P], ["g_rd" + P]),
            A("dve", lambda e, t=t: e.tensor_scalar_mul(t["v"][:], t["pr"][:],
                                                        t["rd"][:, 0:1]),
              ["g_pr" + P, "g_rd" + P], ["g_v" + P]),
            A("dve", lambda e, t=t: e.reduce_max(t["m1"][:], t["v"][:],
                                                 axis=AX.X),
              ["g_v" + P], ["g_m1" + P]),
            A("dve", lambda e, t=t: e.tensor_single_scalar(
                t["sel"][:], t["v"][:], t["m1"][:, 0:1], ALU.is_ge),
              ["g_v" + P, "g_m1" + P], ["g_sel" + P]),
            A("dve", lambda e, t=t: e.reduce_sum(t["n1"][:], t["sel"][:],
                                                 axis=AX.X),
              ["g_sel" + P], ["g_n1" + P]),
            A("dve", lambda e, t=t: e.scalar_tensor_tensor(
                t["ex"][:], t["sel"][:], -1e9, t["v"][:], ALU.mult, ALU.add),
              ["g_sel" + P, "g_v" + P], ["g_ex" + P]),
            A("dve", lambda e, t=t: e.reduce_max(t["m2"][:], t["ex"][:],
                                                 axis=AX.X),
              ["g_ex" + P], ["g_m2" + P]),
            A("dve", lambda e, t=t: e.tensor_single_scalar(
                t["tie"][:], t["n1"][:], 1.5, ALU.is_ge),
              ["g_n1" + P], ["g_tie" + P]),
            A("dve", lambda e, t=t: e.tensor_sub(t["den"][:], t["m1"][:],
                                                 t["m2"][:]),
              ["g_m1" + P, "g_m2" + P], ["g_den" + P]),
            A("dve", lambda e, t=t: e.scalar_tensor_tensor(
                t["m2"][:], t["den"][:], t["tie"][:, 0:1], t["m2"][:],
                ALU.mult, ALU.add),
              ["g_den" + P, "g_tie" + P, "g_m2" + P], ["g_m2" + P]),
            A("dve", lambda e, t=t: e.tensor_scalar(
                t["den"][:], t["m1"][:], t["m2"][:, 0:1], 1e-9, ALU.add,
                ALU.add),
              ["g_m1" + P, "g_m2" + P], ["g_den" + P]),
            A("dve", lambda e, t=t: e.reciprocal(t["rd"][:], t["den"][:]),
              ["g_den" + P], ["g_rd" + P]),
            A("dve", lambda e, t=t: e.tensor_single_scalar(
                t["sel"][:], t["v"][:], t["m2"][:, 0:1], ALU.is_ge),
              ["g_v" + P, "g_m2" + P], ["g_sel" + P]),
            A("dve", lambda e, t=t: e.tensor_mul(t["v"][:], t["v"][:],
                                                 t["sel"][:]),
              ["g_v" + P, "g_sel" + P], ["g_v" + P]),
            A("dve", lambda e, t=t: e.tensor_scalar_mul(
                t["cb"][:], t["v"][:], t["rd"][:, 0:1]),
              ["g_v" + P, "g_rd" + P], ["g_cb" + P]),
            A("pe", lambda e, t=t, p=p: e.transpose(
                trps[p][:], t["cb"][:], ident),
              ["g_cb" + P, "consts"], ["ps_tr" + P]),
            A("dve", lambda e, tc=tc, p=p: e.tensor_copy(
                combT[:, tc * 128:(tc + 1) * 128], trps[p][:]),
              ["ps_tr" + P], ["combT"]),
        ]
        return ops

    for tcp in range(0, NTC, 2):
        a_ops = gate_chain(tcp)
        b_ops = gate_chain(tcp + 1)
        n = max(len(a_ops), len(b_ops))
        for i in range(n):
            if i < len(a_ops):
                a_ops[i]()
            if i < len(b_ops):
                b_ops[i]()
    s.flush("gate")
    for g in reversed(g_tr):
        g.__exit__(None, None, None)
    for g in reversed(g_lg):
        g.__exit__(None, None, None)
    for g in reversed(g_gate):
        g.__exit__(None, None, None)

    NPASS = 16
    WRN = 3
    g_m = (
        [nc.sbuf_tensor("qr%d" % i, [128, T], BF16) for i in range(2)] +
        [nc.sbuf_tensor("kr%d" % i, [128, T], BF16) for i in range(2)] +
        [nc.sbuf_tensor("tab%d" % i, [128, T], BF16) for i in range(4)] +
        [nc.sbuf_tensor("vaug%d" % i, [128, 2 * 16 * 65], BF16) for i in range(2)] +
        [nc.sbuf_tensor("Psb%d" % i, [128, 512], BF16) for i in range(4)] +
        [nc.sbuf_tensor("opair%d" % i, [128, T], BF16) for i in range(2)] +
        [nc.sbuf_tensor("wsf%d" % i, [128, 1024], F32) for i in range(3)] +
        [nc.sbuf_tensor("qkb%d" % i, [128, 1024], BF16) for i in range(2)] +
        [nc.sbuf_tensor("wvb", [128, 1024], BF16)] +
        [nc.sbuf_tensor("woutb%d" % i, [128, 1024], BF16) for i in range(2)] +
        [nc.sbuf_tensor("woq%d" % i, [128, 1024], BF16) for i in range(2)] +
        [nc.sbuf_tensor("w1b%d" % i, [128, 1024], BF16) for i in range(4)] +
        [nc.sbuf_tensor("cbs%d" % i, [128, T], BF16) for i in range(2)] +
        [nc.sbuf_tensor("asf%d" % i, [128, 1024], BF16) for i in range(2)] +
        [nc.sbuf_tensor("sgf%d" % i, [128, 1024], BF16) for i in range(2)] +
        [nc.sbuf_tensor("tsf%d" % i, [128, 1024], BF16) for i in range(2)] +
        [nc.sbuf_tensor("b1h", [128, 256], F32),
         nc.sbuf_tensor("rss", [1, 512], F32),
         nc.sbuf_tensor("gsum", [128, T], F32),
         nc.sbuf_tensor("gsb0", [128, T], BF16),
         nc.sbuf_tensor("gsb1", [128, T], BF16),
         nc.sbuf_tensor("ksqB0", [64, 512], BF16),
         nc.sbuf_tensor("ksqB1", [64, 512], BF16),
         nc.sbuf_tensor("nsc", [128, 8], F32),
         nc.sbuf_tensor("nln", [128, 2], F32),
         nc.sbuf_tensor("ropet", [128, 512], BF16)]
    )
    hm = [g.__enter__() for g in g_m]
    qr, kr, tabs, vaug = hm[0:2], hm[2:4], hm[4:8], hm[8:10]
    Psb, opair, wsf = hm[10:14], hm[14:16], hm[16:19]
    qkb, wvb = hm[19:21], hm[21]
    woutb, woqx, w1b = hm[22:24], hm[24:26], hm[26:30]
    cbs, asf, sgf, tsf = hm[30:32], hm[32:34], hm[34:36], hm[36:38]
    (b1h, rs_s, gsum, gsb0, gsb1, ksqB0, ksqB1, nsc, nln, t1) = hm[38:48]
    ksqX = [ksqB0, ksqB1]
    gsbx = [gsb0, gsb1]

    g_ps = [nc.psum_tensor("sps%d" % i, [128, 512], F32) for i in range(3)] + \
           [nc.psum_tensor("ops%d" % i, [65, 512], F32) for i in range(2)] + \
           [nc.psum_tensor("pjps", [128, 512], F32),
            nc.psum_tensor("hpa", [128, 512], F32),
            nc.psum_tensor("hpb", [128, 512], F32)]
    (sps0, sps1, sps2, ops0, ops1, pjps, hpa, hpb) = [g.__enter__() for g in g_ps]
    sps = [sps0, sps1, sps2]
    opsb = [ops0, ops1]
    hpx = [hpa, hpb]
    vwps = pjps
    ones64x128 = consts[0:64, C_ONE:C_ONE + 128]

    for i in range(4):
        s.dma(lambda e, i=i: e.dma_start(tabs[i][:], d_tab[i]),
              writes=["tab%d" % i])
    for dc in range(NDC):
        s.add("pool", lambda e, dc=dc: e.memset(acc[dc][:], 0.0),
              writes=["acc%d" % dc])
    s.add("dve", lambda e: e.tensor_scalar_mul(b1h[:], b1_all[:], 0.5),
          reads=["b1all"], writes=["b1h"])

    wring = [0]

    def wload(dst_ap, dst_tok, src_ap, tag, eng="pool"):
        i = wring[0] % WRN
        wring[0] += 1
        s.dma(lambda e, i=i, src_ap=src_ap: e.dma_start(wsf[i][:], src_ap),
              writes=["wsf%d" % i], tag=tag)
        if eng == "act":
            s.add("act", lambda e, i=i, dst_ap=dst_ap: e.activation(
                dst_ap, wsf[i][:], AF.Identity), reads=["wsf%d" % i],
                writes=[dst_tok])
        else:
            s.add("pool", lambda e, i=i, dst_ap=dst_ap: e.tensor_copy(
                dst_ap, wsf[i][:]), reads=["wsf%d" % i], writes=[dst_tok])

    pend = []

    def flush_pend(n=None):
        todo = pend[:n] if n else pend[:]
        del pend[:len(todo)]
        for fn in todo:
            fn()

    def gen_ffn():
        gh = [0]
        wload(woqx[0][:], "woq0", d_wf[0], "woqL0")
        for ex in range(E):
            cbp = ex % 2
            for tt in range(NTT):
                sl = slice(tt * 512, (tt + 1) * 512)
                cbk = tt % 2
                s.add("pe", lambda e, ex=ex, sl=sl, cbk=cbk: e.matmul(
                    hpx[cbk][:], sel8[:, ex * 128:(ex + 1) * 128],
                    combT[0:8, sl], start=True, stop=True),
                    reads=["combT", "consts"], writes=["ps_hp%d" % cbk])
                yield
                s.add("act", lambda e, cbp=cbp, sl=sl, cbk=cbk: e.activation(
                    cbs[cbp][:, sl], hpx[cbk][:], AF.Identity, scale=0.5),
                    reads=["ps_hp%d" % cbk], writes=["cbs%d" % cbp])
            s.dma(lambda e, ex=ex, cbp=cbp: e.dma_start(d_cbs[ex], cbs[cbp][:]),
                  reads=["cbs%d" % cbp], writes=["cbsout%d" % ex], tag="cbo%d" % ex)
        for p in range(NPASS):
            for ex in range(E):
                cbp = ex % 2
                s.dma(lambda e, ex=ex, cbp=cbp: e.dma_start(
                    cbs[cbp][:], d_cbs[ex]),
                    reads=["cbsout%d" % ex], writes=["cbs%d" % cbp],
                    tag="cbi_%d_%d" % (p, ex))
                ia = (2 * (p * E + ex)) % 4
                ib = ia + 1
                wload(w1b[ia][:], "w1b%d" % ia, d_w1[ex, p], "w1a_%d_%d" % (ex, p))
                wload(w1b[ib][:], "w1b%d" % ib, d_w1[ex, 16 + p],
                      "w1b_%d_%d" % (ex, p))
                b1a = b1_all[:, ex * 32 + p:ex * 32 + p + 1]
                b1b = b1h[:, ex * 32 + 16 + p:ex * 32 + 16 + p + 1]
                for hf in range(2):
                    hpar = gh[0] % 2
                    gh[0] += 1
                    for tq in range(2):
                        tt = 2 * hf + tq
                        sl = slice(tt * 512, (tt + 1) * 512)
                        qsl = slice(tq * 512, (tq + 1) * 512)
                        for cc in range(NDC):
                            s.add("pe", lambda e, ia=ia, cc=cc, sl=sl: e.matmul(
                                hpa[:], w1b[ia][:, cc * 128:(cc + 1) * 128],
                                xn[cc][:, sl], start=(cc == 0), stop=(cc == NDC - 1)),
                                reads=["w1b%d" % ia, "xn%d" % cc], writes=["ps_hp0"])
                            yield
                        s.add("dve", lambda e, sl=sl, qsl=qsl, hpar=hpar, b=b1a,
                              cbp=cbp: e.scalar_tensor_tensor(
                                  asf[hpar][:, qsl], hpa[:], b, cbs[cbp][:, sl],
                                  ALU.add, ALU.mult),
                              reads=["ps_hp0", "b1all", "cbs%d" % cbp],
                              writes=["as%d_%d" % (hpar, tq)])
                        flush_pend(1)
                        for cc in range(NDC):
                            s.add("pe", lambda e, ib=ib, cc=cc, sl=sl: e.matmul(
                                hpb[:], w1b[ib][:, cc * 128:(cc + 1) * 128],
                                xn[cc][:, sl], start=(cc == 0), stop=(cc == NDC - 1)),
                                reads=["w1b%d" % ib, "xn%d" % cc], writes=["ps_hp1"])
                            yield
                        s.add("act", lambda e, qsl=qsl, hpar=hpar, b=b1b:
                              e.activation(sgf[hpar][:, qsl], hpb[:], AF.Tanh,
                                           bias=b, scale=0.5),
                              reads=["ps_hp1", "b1h"],
                              writes=["sg%d_%d" % (hpar, tq)])
                        flush_pend(1)
                    hsl = slice(hf * 1024, (hf + 1) * 1024)
                    rd = ["as%d_%d" % (hpar, q) for q in range(2)] + \
                         ["sg%d_%d" % (hpar, q) for q in range(2)]
                    if ex == 0:
                        pend.append(lambda hsl=hsl, hpar=hpar, rd=rd, hf=hf: s.add(
                            "dve", lambda e: e.scalar_tensor_tensor(
                                gsum[:, hsl], sgf[hpar][:], 1.0, asf[hpar][:],
                                ALU.add, ALU.mult),
                            reads=rd, writes=["gs%d" % hf]))
                    elif ex < E - 1:
                        pend.append(lambda hpar=hpar, rd=rd: s.add(
                            "dve", lambda e: e.scalar_tensor_tensor(
                                tsf[hpar][:], sgf[hpar][:], 1.0, asf[hpar][:],
                                ALU.add, ALU.mult),
                            reads=rd, writes=["ts%d" % hpar]))
                        pend.append(lambda hsl=hsl, hpar=hpar, hf=hf: s.add(
                            "dve", lambda e: e.tensor_add(
                                gsum[:, hsl], tsf[hpar][:], gsum[:, hsl]),
                            reads=["ts%d" % hpar, "gs%d" % hf],
                            writes=["gs%d" % hf]))
                    else:
                        pend.append(lambda hpar=hpar, rd=rd: s.add(
                            "dve", lambda e: e.scalar_tensor_tensor(
                                tsf[hpar][:], sgf[hpar][:], 1.0, asf[hpar][:],
                                ALU.add, ALU.mult),
                            reads=rd, writes=["ts%d" % hpar]))
                        gx = p % 2
                        pend.append(lambda hsl=hsl, hpar=hpar, hf=hf, gx=gx: s.add(
                            "dve", lambda e: e.tensor_add(
                                gsbx[gx][:, hsl], tsf[hpar][:], gsum[:, hsl]),
                            reads=["ts%d" % hpar, "gs%d" % hf],
                            writes=["gq%d_%d" % (gx, hf)]))
            flush_pend()
            if p % 2 == 1:
                last = (p == NPASS - 1)
                for tt in range(NTT):
                    for dc in range(NDC):
                        sl = slice(tt * 512, (tt + 1) * 512)
                        ep = dc % 2
                        for gx in range(2):
                            s.add("pe", lambda e, dc=dc, sl=sl, ep=ep, gx=gx,
                                  last=last: e.matmul(
                                      hpx[ep][:],
                                      woqx[gx][:, dc * 128:(dc + 1) * 128],
                                      gsbx[gx][:, sl], start=(gx == 0),
                                      stop=(gx == 1 and not last)),
                                  reads=["woq%d" % gx, "gq%d_%d" % (gx, tt // 2)],
                                  writes=["ps_hp%d" % ep])
                            yield
                        if last:
                            s.add("pe", lambda e, dc=dc, sl=sl, ep=ep: e.matmul(
                                hpx[ep][:], boRep[0:8, dc * 128:(dc + 1) * 128],
                                combT[0:8, sl], start=False, stop=True),
                                reads=["boRep", "combT"], writes=["ps_hp%d" % ep])
                            yield
                        s.add("dve", lambda e, dc=dc, sl=sl, ep=ep: e.tensor_add(
                            acc[dc][:, sl], hpx[ep][:], acc[dc][:, sl]),
                            reads=["ps_hp%d" % ep, "acc%d" % dc],
                            writes=["acc%d" % dc])
            if p < NPASS - 1:
                wload(woqx[(p + 1) % 2][:], "woq%d" % ((p + 1) % 2),
                      d_wf[p + 1], "woq%d" % (p + 1))

    genf = gen_ffn()

    _DONE = object()

    def fill(n):
        for _ in range(n):
            if next(genf, _DONE) is _DONE:
                break

    def prologue_loads(c):
        wload(qkb[0][:], "qkb0", d_qk[c], "wq%d" % c)
        wload(qkb[1][:], "qkb1", d_qk[8 + c], "wk%d" % c)
        wload(wvb[:], "wvb", d_wv[c], "wv%d" % c)

    def prologue(c):
        r = c % 2
        for (wi, dst, dtok) in ((0, qr[r], "qr%d" % r), (1, kr[r], "kr%d" % r)):
            for tt in range(NTT):
                sl = slice(tt * 512, (tt + 1) * 512)
                for cc in range(NDC):
                    s.add("pe", lambda e, wi=wi, cc=cc, sl=sl: e.matmul(
                        pjps[:], qkb[wi][:, cc * 128:(cc + 1) * 128],
                        xn[cc][:, sl], start=(cc == 0), stop=(cc == NDC - 1)),
                        reads=["qkb%d" % wi, "xn%d" % cc], writes=["ps_pj"])
                fill(2)
                s.add("act", lambda e, dst=dst, sl=sl: e.activation(
                    dst[:, sl], pjps[:], AF.Identity),
                    reads=["ps_pj"], writes=[dtok])
                fill(1)
        for (dst, dtok, tc_, ts_) in ((qr[r], "qr%d" % r, 0, 1),
                                      (kr[r], "kr%d" % r, 2, 3)):
            for tt in range(NTT):
                sl = slice(tt * 512, (tt + 1) * 512)
                fill(1)
                s.add("pe", lambda e, dst=dst, sl=sl: e.matmul(
                    pjps[:], rotT, dst[:, sl], start=True, stop=True),
                    reads=[dtok, "consts"], writes=["ps_pj"])
                s.add("dve", lambda e, dst=dst, sl=sl, tc_=tc_:
                      e.tensor_mul(t1[:], dst[:, sl], tabs[tc_][:, sl]),
                      reads=[dtok, "tab%d" % tc_], writes=["ropet"])
                s.add("dve", lambda e, sl=sl, ts_=ts_:
                      e.tensor_mul(pjps[:], pjps[:], tabs[ts_][:, sl]),
                      reads=["ps_pj", "tab%d" % ts_], writes=["ps_pj"])
                s.add("dve", lambda e, dst=dst, sl=sl: e.tensor_add(
                    dst[:, sl], t1[:], pjps[:]),
                    reads=["ropet", "ps_pj"], writes=[dtok])
                fill(1)
        for hh in (0, 1):
            h = 2 * c + hh
            hl = hh * 64
            for (stile, stok, col0) in ((qr[r], "qr%d" % r, 0),
                                        (kr[r], "kr%d" % r, 4)):
                for lt in range(NTT):
                    sl = slice(lt * 512, (lt + 1) * 512)
                    kb = lt % 2
                    s.add("act", lambda e, stile=stile, hl=hl, sl=sl, kb=kb:
                          e.activation(
                              ksqX[kb][:], stile[hl:hl + 64, sl], AF.Square),
                          reads=[stok], writes=["ksqB%d" % kb])
                    fill(1)
                    s.add("pe", lambda e, kb=kb: e.matmul(
                        pjps[:], ones64x128, ksqX[kb][:], start=True, stop=True),
                        reads=["ksqB%d" % kb, "consts"], writes=["ps_pj"])
                    s.add("dve", lambda e, col=col0 + lt: e.reduce_max(
                        nsc[:, col:col + 1], pjps[:], axis=AX.X),
                        reads=["ps_pj"], writes=["nsc%d" % (col0 + lt)])
            s.add("dve", lambda e: e.reduce_max(nln[:, 0:1], nsc[:, 0:4], axis=AX.X),
                  reads=["nsc0", "nsc1", "nsc2", "nsc3"], writes=["nl0"])
            s.add("dve", lambda e: e.reduce_max(nln[:, 1:2], nsc[:, 4:8], axis=AX.X),
                  reads=["nsc4", "nsc5", "nsc6", "nsc7"], writes=["nl1"])
            s.add("dve", lambda e: e.tensor_scalar_mul(
                nln[:, 0:1], nln[:, 0:1], -4.0),
                reads=["nl0"], writes=["nl0"])
            s.add("dve", lambda e, h=h: e.scalar_tensor_tensor(
                bias16[:, h:h + 1], nln[:, 1:2], -0.0625, nln[:, 0:1],
                ALU.mult, ALU.add),
                reads=["nl0", "nl1"], writes=["bias%d" % h])
        ones_ap = vaug[r][:, :].rearrange("p (g t s) -> p g t s", g=2, t=16,
                                          s=65)[:, :, :, 64:65]
        s.add("pool", lambda e, ones_ap=ones_ap: e.memset(ones_ap, 1.0),
              writes=["vaug%d" % r])
        for tc in range(NTC):
            fill(1)
            for cc in range(NDC):
                s.add("pe", lambda e, tc=tc, cc=cc: e.matmul(
                    sps[2][:, 0:128], xn[cc][:, tc * 128:(tc + 1) * 128],
                    wvb[:, cc * 128:(cc + 1) * 128],
                    start=(cc == 0), stop=(cc == NDC - 1)),
                    reads=["wvb", "xn%d" % cc], writes=["ps_s2"])
            fill(2)
            dstv = vaug[r][:, :].rearrange("p (g rr) -> p g rr", g=2, rr=1040)[
                :, :, tc * 65:tc * 65 + 64]
            srcv = sps[2][:, 0:128].rearrange("p (g x) -> p g x", g=2, x=64)
            s.add("act", lambda e, dstv=dstv, srcv=srcv: e.activation(
                dstv, srcv, AF.Identity),
                reads=["ps_s2"], writes=["vaug%d" % r])
            fill(1)

    apend = []

    def flush_anorm():
        while apend:
            apend.pop(0)[1]()

    def FILLQ(mc):
        fill(6)

    def head_lt(h, lt):
        c, hl = h // 2, (h % 2) * 64
        r, vp, hg = c % 2, c % 2, h % 2
        qtok, ktok = "qr%d" % r, "kr%d" % r
        sl = slice(lt * 512, (lt + 1) * 512)
        ob = lt % 2

        def _smm(mc):
            sb = mc % 4
            spb = sps[sb] if sb < 3 else pjps
            stok = ("ps_s%d" % sb) if sb < 3 else "ps_pj"
            s.add("pe", lambda e, mc=mc, spb=spb, stok=stok: e.matmul(
                spb[:], kr[r][hl:hl + 64, mc * 128:(mc + 1) * 128],
                qr[r][hl:hl + 64, sl], start=True, stop=True),
                reads=[ktok, qtok], writes=[stok])
            s.add("act", lambda e, sb=sb, spb=spb, stok=stok: e.activation(
                Psb[sb][:], spb[:], AF.Exp, bias=bias16[:, h:h + 1]),
                reads=[stok, "bias%d" % h], writes=["Psb%d" % sb])

        _smm(0)
        _smm(1)
        _smm(2)
        _smm(3)
        for mc in range(NTC):
            pb = mc % 4
            FILLQ(mc)
            s.add("pe", lambda e, mc=mc, pb=pb: e.matmul(
                opsb[ob][:],
                vaug[vp][:, hg * 1040 + mc * 65:hg * 1040 + mc * 65 + 65],
                Psb[pb][:], start=(mc == 0), stop=(mc == NTC - 1)),
                reads=["vaug%d" % vp, "Psb%d" % pb], writes=["ps_o%d" % ob])
            if mc + 4 < NTC:
                _smm(mc + 4)
            if mc == 1:
                while apend and apend[0][0] == 0:
                    apend.pop(0)[1]()
            if mc == 13:
                flush_anorm()
        apend.append((0, lambda ob=ob: s.add(
            "dve", lambda e: e.reciprocal(rs_f[:], opsb[ob][64:65, :]),
            reads=["ps_o%d" % ob], writes=["rsf"])))
        apend.append((1, lambda: s.add(
            "act", lambda e: e.activation(rs_b[:], rs_f[:], AF.Identity),
            reads=["rsf"], writes=["rsb"])))
        apend.append((1, lambda: s.add(
            "pe", lambda e: e.matmul(pjps[0:64, :], onesrow64, rs_b[:],
                                     start=True, stop=True),
            reads=["rsb", "consts"], writes=["ps_pj"])))
        apend.append((1, lambda: s.add(
            "act", lambda e: e.activation(bc_sb[:], pjps[0:64, :], AF.Identity),
            reads=["ps_pj"], writes=["bcsb"])))
        apend.append((1, lambda c=c, hl=hl, sl=sl, ob=ob: s.add(
            "dve", lambda e: e.tensor_mul(
                opair[c % 2][hl:hl + 64, sl], opsb[ob][0:64, :], bc_sb[:]),
            reads=["ps_o%d" % ob, "bcsb"], writes=["opair%d" % (c % 2)])))

    def outproj(c):
        flush_anorm()
        for tt in range(NTT):
            for dc in range(NDC):
                sl = slice(tt * 512, (tt + 1) * 512)
                op_ = dc % 2
                s.add("pe", lambda e, c=c, dc=dc, sl=sl, op_=op_: e.matmul(
                    sps[op_][:], woutb[c % 2][:, dc * 128:(dc + 1) * 128],
                    opair[c % 2][:, sl], start=True, stop=True),
                    reads=["woutb%d" % (c % 2), "opair%d" % (c % 2)],
                    writes=["ps_s%d" % op_])
                fill(2)
                s.add("dve", lambda e, dc=dc, sl=sl, op_=op_: e.tensor_add(
                    acc[dc][:, sl], sps[op_][:], acc[dc][:, sl]),
                    reads=["ps_s%d" % op_, "acc%d" % dc], writes=["acc%d" % dc])
                fill(1)

    prologue_loads(0)
    prologue(0)
    for c in range(NDC):
        wload(woutb[c % 2][:], "woutb%d" % (c % 2), d_wo[c], "wout%d" % c)
        if c + 1 < NDC:
            prologue_loads(c + 1)
        for lt in range(NTT):
            head_lt(2 * c, lt)
        if c + 1 < NDC:
            prologue(c + 1)
        for lt in range(NTT):
            head_lt(2 * c + 1, lt)
        outproj(c)
    for _ in genf:
        pass
    flush_pend()
    s.flush("merged")
    for g in reversed(g_ps):
        g.__exit__(None, None, None)
    for g in reversed(g_m):
        g.__exit__(None, None, None)

    if DEBUG_OUT and LAST_PHASE in ("qkv", "attn", "ffn"):
        _debug_dump(nc, s, dbg_out, {"acc": acc})
        nc.finalize()
        return nc


    g_fin = [nc.sbuf_tensor("fxf%d" % i, [128, T], F32) for i in range(4)] + \
            [nc.sbuf_tensor("fout%d" % i, [128, T], F32) for i in range(4)]
    fxf = [g.__enter__() for g in g_fin[0:4]]
    fout = [g.__enter__() for g in g_fin[4:8]]
    for dc in range(NDC):
        p = dc % 4
        s.dma(lambda e, dc=dc, p=p: e.dma_start(
            fxf[p][:], d_xT[dc * 128:(dc + 1) * 128, :]),
            reads=["fout%d" % p], writes=["fxf%d" % p])
        s.add("dve", lambda e, dc=dc, p=p: e.tensor_add(
            fout[p][:], fxf[p][:], acc[dc][:]),
            reads=["fxf%d" % p, "acc%d" % dc], writes=["fout%d" % p])
        s.dma(lambda e, dc=dc, p=p: e.dma_start(
            d_out[dc * 128:(dc + 1) * 128, :], fout[p][:]),
            reads=["fout%d" % p], writes=["out%d" % dc], eng="pool")
    s.add("pool", lambda e: e.nop(),
          reads=["out%d" % dc for dc in range(NDC)], tag="fence")
    s.flush("final")
    for g in reversed(g_fin):
        g.__exit__(None, None, None)

    nc.finalize()
    return nc


def _emit_vpair(s, nc, pair, d_wv, wvf, wvb, vaug, xn, vps):
    p = pair % 2
    s.dma(lambda e, pair=pair, p=p: e.dma_start(wvf[p][:], d_wv[pair]),
          reads=["wvf%d" % p], writes=["wvf%d" % p], tag="wv%d" % pair)
    s.add("pool", lambda e, p=p: e.tensor_copy(wvb[p][:], wvf[p][:]),
          reads=["wvf%d" % p], writes=["wvb%d" % p])
    ones_ap = vaug[p][:, :].rearrange("p (g t s) -> p g t s", g=2, t=16, s=65)[
        :, :, :, 64:65]
    s.add("pool", lambda e, ones_ap=ones_ap: e.memset(ones_ap, 1.0),
          writes=["vaug%d" % p])
    for tc in range(NTC):
        for c in range(NDC):
            s.add("pe", lambda e, p=p, tc=tc, c=c: e.matmul(
                vps[:], xn[c][:, tc * 128:(tc + 1) * 128],
                wvb[p][:, c * 128:(c + 1) * 128],
                start=(c == 0), stop=(c == NDC - 1)),
                reads=["wvb%d" % p, "xn%d" % c], writes=["ps_v"])
        dst = vaug[p][:, :].rearrange("p (g r) -> p g r", g=2, r=1040)[
            :, :, tc * 65:tc * 65 + 64]
        src = vps[:].rearrange("p (g x) -> p g x", g=2, x=64)
        s.add("act", lambda e, dst=dst, src=src: e.activation(
            dst, src, AF.Identity),
            reads=["ps_v"], writes=["vaug%d" % p])


def _debug_dump(nc, s, dbg_out, tensors):
    row = 0
    guards = [nc.sbuf_tensor("dbgcvt", [128, 2560], F32)]
    cvt = guards[0].__enter__()
    for name, tiles in tensors.items():
        for t in tiles:
            npart = t[:].shape[0]
            fw = t[:].shape[-1]
            s.add("dve", lambda e, t=t, npart=npart, fw=fw: e.tensor_copy(
                cvt[0:npart, 0:fw], t[:]), reads=["dbgcvt"], writes=["dbgcvt"])
            s.dma(lambda e, row=row, npart=npart, fw=fw: e.dma_start(
                dbg_out[row:row + npart, 0:fw], cvt[0:npart, 0:fw]),
                reads=["dbgcvt"], writes=["dbg"])
            row += npart
    s.add("pool", lambda e: e.nop(), reads=["dbg"], tag="fence")
    s.flush("debug")
    for g in reversed(guards):
        g.__exit__(None, None, None)



def _host_prep(L0_attn_x, L0_attn_w_qkv, L0_attn_w_out, L0_ffn_Wg, L0_ffn_W1,
               L0_ffn_b1, L0_ffn_Wo, L0_ffn_bo):
    x = np.asarray(L0_attn_x, dtype=np.float32)
    wqkv = np.asarray(L0_attn_w_qkv, dtype=np.float32)
    wout = np.asarray(L0_attn_w_out, dtype=np.float32)
    Wg = np.asarray(L0_ffn_Wg, dtype=np.float32)
    W1 = np.asarray(L0_ffn_W1, dtype=np.float32)
    b1 = np.asarray(L0_ffn_b1, dtype=np.float32)
    Wo = np.asarray(L0_ffn_Wo, dtype=np.float32)
    bo = np.asarray(L0_ffn_bo, dtype=np.float32)

    xT = np.ascontiguousarray(x.transpose(0, 2, 1))
    WT = np.ascontiguousarray(wqkv.T)
    wqkS = np.ascontiguousarray(
        WT[:, :2048].reshape(8, 128, 16, 128).transpose(2, 1, 0, 3)
    ).reshape(16, 128, 1024)
    wvS = np.ascontiguousarray(
        WT[:, 2048:].reshape(8, 128, 8, 128).transpose(2, 1, 0, 3)
    ).reshape(8, 128, 1024)
    woutS = np.ascontiguousarray(
        wout.T.reshape(8, 128, 8, 128)).reshape(8, 128, 1024)
    wgS = np.ascontiguousarray(Wg.T.reshape(8, 128, 8).transpose(1, 0, 2)
                               ).reshape(128, 64)
    w1S = np.ascontiguousarray(
        W1.transpose(0, 2, 1).reshape(E, 8, 128, 32, 128).transpose(0, 3, 2, 1, 4)
    ).reshape(E, 32, 128, 1024)
    wfS = np.ascontiguousarray(Wo.T.reshape(16, 128, 8, 128)).reshape(16, 128, 1024)
    b1S = np.ascontiguousarray(b1.reshape(8, 32, 128).transpose(2, 0, 1)
                               ).reshape(128, 256)
    boS = np.ascontiguousarray(np.broadcast_to(bo, (8, 1024)))

    consts = np.zeros((128, 1410), dtype=np.float32)
    consts[:, 0:128] = np.eye(128)
    R = np.zeros((64, 64), dtype=np.float32)
    for j in range(32):
        R[j, j + 32] = -1.0
        R[j + 32, j] = 1.0
    R2 = np.zeros((128, 128), dtype=np.float32)
    R2[0:64, 0:64] = R
    R2[64:128, 64:128] = R
    consts[:, 128:256] = R2.T
    consts[:, 256:384] = 1.0
    consts[0:64, 384] = 1.0
    consts[64:128, 385] = 1.0
    for ex in range(8):
        consts[ex, 386 + ex * 128:386 + (ex + 1) * 128] = 1.0
    constS = consts.astype(BF)

    half = 32
    inv = 1.0 / (10000.0 ** (np.arange(half, dtype=np.float64) / half))
    ang = np.arange(T, dtype=np.float64)[:, None] * inv[None, :]
    cos = np.cos(ang)
    sin = np.sin(ang)
    cos64 = np.concatenate([cos, cos], axis=1).T
    sin64 = np.concatenate([sin, sin], axis=1).T
    cos128 = np.concatenate([cos64, cos64], axis=0)
    sin128 = np.concatenate([sin64, sin64], axis=0)
    tabS = np.stack([
        cos128 * 0.125, sin128 * 0.125, cos128, sin128
    ]).astype(BF)

    shared = {"wqkS": wqkS, "wvS": wvS, "woutS": woutS, "wgS": wgS,
              "w1S": w1S, "wfS": wfS, "b1S": b1S, "boS": boS,
              "constS": constS, "tabS": tabS}
    in_maps = [dict(shared, xT=np.ascontiguousarray(xT[c])) for c in range(8)]
    return in_maps


def kernel(**inputs):
    in_maps = _host_prep(**inputs)
    nc = build_program()
    res = run_bass_kernel_spmd(nc, in_maps, core_ids=list(range(8)))
    outs = []
    for c in range(8):
        key = "dbgT" if DEBUG_OUT else "outT"
        o = res.results[c][key]
        outs.append(o if DEBUG_OUT else np.ascontiguousarray(o.T))
    return np.stack(outs).astype(np.float32)
